# version 21
# baseline (speedup 1.0000x reference)
import contextlib
import numpy as np
import ml_dtypes
import concourse.bass as bass
import concourse.mybir as mybir
from concourse.bass_utils import run_bass_kernel_spmd

F32 = mybir.dt.float32
BF16 = mybir.dt.bfloat16
I32 = mybir.dt.int32
AF = mybir.ActivationFunctionType
ALU = mybir.AluOpType
AX = mybir.AxisListType
BF = ml_dtypes.bfloat16

S, DM, HID = 8192, 1024, 3072
NS = 16
QR = NS * 128
DC = DM // 128
HC = HID // 128
SCALE = 1.0 / 127.0
EPS = 1e-5
NEG = -30000.0
SELF_SYNC = set()
DBG_STATS = False
FAST_OK = set()


class Emit:

    def __init__(self, nc):
        self.nc = nc
        self.es = contextlib.ExitStack()
        self.sem = {}
        self.cnt = {}
        self.qs = {k: [] for k in ("sy", "pl", "v", "a", "t")}
        for name in ("v", "a", "t"):
            self._mk(name)

    def _mk(self, name):
        if name not in self.sem:
            self.sem[name] = self.es.enter_context(
                self.nc.semaphore("sem_" + name))
            self.cnt[name] = 0
        return self.sem[name]

    def qwait(self, ek, semkey, val=None):
        self._mk(semkey)
        v = self.cnt[semkey] if val is None else val
        self.qs[ek].append(("w", self.sem[semkey], v))

    def qop(self, ek, fn, inckey=None):
        mark = None
        if inckey is not None:
            self._mk(inckey)
            self.cnt[inckey] += 1
            mark = self.cnt[inckey]
            self.qs[ek].append(("o", fn, self.sem[inckey], 1))
        else:
            self.qs[ek].append(("o", fn, None, 0))
        return mark

    def qsop(self, ek, fn=None, fast=False, dep=None):
        key = {"v": "v", "a": "a", "t": "t"}[ek]
        if key == "v" and (fast is False or fast not in FAST_OK):
            if not (isinstance(dep, int) and dep < 0):
                self.qwait(ek, key, dep)
        return self.qop(ek, fn, key)

    def qdma(self, ek, semkey, fn):
        self._mk(semkey)
        self.cnt[semkey] += 16
        mark = self.cnt[semkey]
        self.qs[ek].append(("o", fn, self.sem[semkey], 16))
        return mark

    def qgroup(self, ek, fns, inckey):
        self._mk(inckey)
        self.cnt[inckey] += 1
        mark = self.cnt[inckey]
        self.qs[ek].append(("g", fns, self.sem[inckey], 1))
        return mark

    def mark(self, key):
        self._mk(key)
        return self.cnt[key]

    def barrier(self):
        marks = dict(self.cnt)
        for ek in self.qs:
            for kk, vv in marks.items():
                self.qs[ek].append(("w", self.sem[kk], vv))

    def flush(self, blk):
        def emit(eng, items):
            for it in items:
                if it[0] == "w":
                    eng.wait_ge(it[1], it[2])
                elif it[0] == "o":
                    ins = it[1](eng)
                    if it[2] is not None:
                        ins.then_inc(it[2], it[3])
                else:
                    for fn in it[1]:
                        ins = fn(eng)
                    ins.then_inc(it[2], it[3])

        @blk.sync
        def _(s):
            emit(s, self.qs["sy"])

        @blk.gpsimd
        def _(g):
            emit(g, self.qs["pl"])

        @blk.vector
        def _(v):
            emit(v, self.qs["v"])

        @blk.scalar
        def _(a):
            emit(a, self.qs["a"])

        @blk.tensor
        def _(t):
            emit(t, self.qs["t"])


def build(nphase=9):
    nc = bass.Bass("TRN2", target_bir_lowering=False, debug=False)

    xk_h = nc.dram_tensor("xk", [S, DM], F32, kind="ExternalInput").ap()
    xq_h = nc.dram_tensor("xq", [QR, DM], F32, kind="ExternalInput").ap()
    wq_h = nc.dram_tensor("wq", [DM, DM], F32, kind="ExternalInput").ap()
    wk_h = nc.dram_tensor("wk", [DM, DM], F32, kind="ExternalInput").ap()
    wv_h = nc.dram_tensor("wv", [DM, DM], F32, kind="ExternalInput").ap()
    wg_h = nc.dram_tensor("wg", [HID, DM], I32, kind="ExternalInput").ap()
    wu_h = nc.dram_tensor("wu", [HID, DM], I32, kind="ExternalInput").ap()
    wd_h = nc.dram_tensor("wd", [DM, HID], I32, kind="ExternalInput").ap()
    mask_h = nc.dram_tensor("mask", [128, 512], F32, kind="ExternalInput").ap()
    ident_h = nc.dram_tensor("ident", [128, 128], BF16, kind="ExternalInput").ap()
    out_h = nc.dram_tensor("out", [QR, DM], F32, kind="ExternalOutput").ap()

    kT_s = nc.dram_tensor("kT_s", [S // 512, 128, DC * 512], BF16,
                          kind="Internal").ap()
    vv_s = nc.dram_tensor("vv_s", [S // 512, 128, 4 * DM], BF16,
                          kind="Internal").ap()
    wgT_s = nc.dram_tensor("wgT_s", [DM, HID], BF16, kind="Internal").ap()
    wuT_s = nc.dram_tensor("wuT_s", [DM, HID], BF16, kind="Internal").ap()
    wdT_s = nc.dram_tensor("wdT_s", [HID, DM], BF16, kind="Internal").ap()

    BASE = 16896

    def at(name, shape, dtype, off):
        return nc.alloc_sbuf_tensor_at(name, shape, dtype, offset=BASE + off)

    wq_b = at("wq_b", [128, DC, DM], BF16, 0)
    wk_b = at("wk_b", [128, DC, DM], BF16, 16384)
    wv_b = at("wv_b", [128, DC, DM], BF16, 32768)
    xnq = at("xnq", [128, DC, QR], BF16, 49152)
    qT = at("qT", [128, DC, QR], BF16, 81920)
    g_b = at("g_b", [128, NS, DM], BF16, 114688)
    ident = at("ident_b", [128, 128], BF16, 147456)
    maskb = at("maskb", [128, 512], F32, 147712)
    bn6 = at("bn6", [128, 2, 12], F32, 149760)
    msb = at("msb", [128, 2, 16], F32, 149824)
    lsb = at("lsb", [128, 2, 16], F32, 149952)
    fsb = at("fsb", [128, 2, 16], F32, 150080)
    mglob = at("mglob", [128, 2], F32, 150208)
    lsum = at("lsum", [128, 2], F32, 150240)
    linv = at("linv", [128, 2], F32, 150272)
    veps = at("veps", [128, 80], F32, 150304)
    STG = 152064

    wst = at("wst", [128, HC, DM], BF16, 81920)
    wts = at("wts", [128, 4, 1536], BF16, 131072)
    xkb4 = at("xkb4", [128, 2, 4, DM], F32, STG)
    xnr = at("xnr", [128, 2, DM], BF16, STG + 32768)
    kst = at("kst", [128, DC, 512], BF16, STG + 36864)
    vst = at("vst", [128, 4, DM], BF16, STG + 45056)
    xns = at("xns", [128, 2, DC, 512], BF16, 49152)
    mvall = at("mvall", [128, 80, 2], F32, 150656)
    rstdall = at("rstdall", [128, 80], F32, 151296)
    wdT_b = at("wdT_b", [128, HC, DM], BF16, 0)
    hact = at("hact", [128, HC, DM], BF16, STG)
    wgu = at("wgu", [128, 2, 2, DC, 128], BF16, STG + 49152)
    silu_t = at("silu_t", [128, DM], BF16, STG + 57344)
    pbufA = at("pbufA", [128, S], BF16, 0)
    pbufB = at("pbufB", [128, S], BF16, 49152)
    kbufX = at("kbufX", [128, 2, DC, 512], BF16, 65536)
    kbuf = at("kbuf", [128, 2, DC, 512], BF16, STG)
    vbuf = at("vbuf", [128, 2, 4, DM], BF16, STG + 16384)
    pT = at("pT", [128, 2, 2, 4, 128], BF16, STG + 32768)
    outt = at("outt", [128, 2, DM], F32, STG + 36864)
    xrs = at("xrs", [128, 2, DM], F32, STG + 45056)
    sg_t = at("sg_t", [128, DM], F32, STG + 53248)

    psA = nc.alloc_psum_tensor("psA", [128, 2048], F32)
    psBC = nc.alloc_psum_tensor("psBC", [128, 1024], F32)
    psD = nc.alloc_psum_tensor("psD", [128, 1024], F32)
    trA = psA.bitcast(BF16)
    trBC = psBC.bitcast(BF16)

    E = Emit(nc)

    for w_h, w_b in ((wk_h, wk_b), (wv_h, wv_b), (wq_h, wq_b)):
        E.qdma("pl", "wqkv", lambda g, w_h=w_h, w_b=w_b: g.dma_start(
            out=w_b[:], in_=w_h.rearrange("(c p) e -> p c e", p=128)))
    E.qdma("sy", "const", lambda s: s.dma_start(out=ident[:], in_=ident_h[:]))
    E.qdma("sy", "const", lambda s: s.dma_start(out=maskb[:], in_=mask_h[:]))
    sp_const = E.mark("const")

    if nphase >= 1:
        wts_marks = []
        wts_dma = []
        wstf = wst.ap().rearrange("p a b -> p (a b)")
        tr_done = []
        for wi, (w_h, wT_s) in enumerate(
                ((wg_h, wgT_s), (wu_h, wuT_s), (wd_h, wdT_s))):
            is_d = wi == 2
            nrow = DC if is_d else HC
            ncol = HC if is_d else DC
            wfree = HID if is_d else DM
            hr = nrow // 2
            for hf in range(2):
                hpar = (wi * 2 + hf) % 2
                r_lo = hf * hr
                if len(tr_done) >= 2:
                    E.qwait("pl", "t", tr_done[-2])
                E.qdma("pl", "wst%d" % hpar, lambda g, w_h=w_h, r_lo=r_lo,
                       hr=hr, wfree=wfree, hpar=hpar: g.dma_start(
                           out=wstf[:, hpar * 12288:
                                    hpar * 12288 + hr * wfree].rearrange(
                               "p (a b) -> p a b", b=wfree),
                           in_=w_h[r_lo * 128:(r_lo + hr) * 128, :].rearrange(
                               "(c p) e -> p c e", p=128)))
                gp_m = E.mark("wst%d" % hpar)

                for cb in range(ncol):
                    nunit = (hr + 7) // 8
                    for u in range(nunit):
                        r0, r1 = u * 8, min(hr, u * 8 + 8)
                        par = (cb * nunit + u) % 4
                        E.qwait("t", "const", sp_const)
                        E.qwait("t", "wst%d" % hpar, gp_m)
                        if len(wts_marks) >= 4:
                            E.qwait("t", "a", wts_marks[-4])
                        mk = E.qgroup("t", [
                            (lambda t, rb=rb, r0=r0, cb=cb, par=par,
                             wfree=wfree, hpar=hpar: t.transpose(
                                 trA[:, par * 1024 + (rb - r0) * 128:
                                     par * 1024 + (rb - r0 + 1) * 128],
                                 wstf[:, hpar * 12288 + rb * wfree + cb * 128:
                                      hpar * 12288 + rb * wfree +
                                      (cb + 1) * 128],
                                 ident[:]))
                            for rb in range(r0, r1)], "t")

                        E.qwait("a", "t")
                        if len(wts_dma) >= 4:
                            E.qwait("a", *wts_dma[-4])
                        wts_marks.append(E.qsop("a", lambda a, r0=r0, r1=r1,
                                                par=par, wpar=cb % 4:
                                                a.activation(
                            out=wts[:, wpar, r0 * 128:r1 * 128],
                            in_=trA[:, par * 1024:
                                    par * 1024 + (r1 - r0) * 128],
                            func=AF.Copy)))

                    E.qwait("a", "a")
                    wts_dma.append(("wTs%d" % (cb % 4), E.qdma(
                        "a", "wTs%d" % (cb % 4), lambda a, cb=cb, hr=hr,
                        r_lo=r_lo, wT_s=wT_s, wpar=cb % 4: a.dma_start(
                            out=wT_s[cb * 128:(cb + 1) * 128,
                                     r_lo * 128:(r_lo + hr) * 128],
                            in_=wts[:, wpar, 0:hr * 128]))))
                tr_done.append(mk)

    if nphase >= 2:
        nq_tiles = 16 if nphase >= 3 else 0

        aggr_m = []

        nmark = []

        def xn_phase(src_h, ntile, kv, base):
            tr_evac = []
            tr_tm = []
            k_evac = []
            v_evac = []
            kv_tmark = []
            E.qwait("t", "a")
            if kv:
                E.qwait("t", "wqkv")
            for b in range(ntile // 4):
                pb = b % 2
                bpar = b % 2
                if len(nmark) >= 2:
                    E.qwait("sy", "v", nmark[-2])
                E.qdma("sy", "xkb4_%d" % bpar,
                       lambda s, b=b, bpar=bpar, src_h=src_h: s.dma_start(
                           out=xkb4[:, bpar, :, :],
                           in_=src_h[b * 512:(b + 1) * 512, :].rearrange(
                               "(ti p) d -> p ti d", p=128)))
                ld_m = E.mark("xkb4_%d" % bpar)

                g4 = base + 4 * b
                E.qwait("v", "xkb4_%d" % bpar, ld_m)
                for ti in range(4):
                    b2 = ti % 2
                    E.qsop("v", lambda v, bpar=bpar, ti=ti, b2=b2: v.bn_stats(
                        bn6[:, b2, 0:6], xkb4[:, bpar, ti, 0:512]),
                        dep=(aggr_m[-2] if len(aggr_m) >= 2 else 0))
                    E.qsop("v", lambda v, bpar=bpar, ti=ti, b2=b2: v.bn_stats(
                        bn6[:, b2, 6:12], xkb4[:, bpar, ti, 512:1024]),
                        dep=-10**9)
                    aggr_m.append(E.qsop("v", lambda v, g4=g4, ti=ti, b2=b2:
                                         v.bn_aggr(
                        mvall[:, g4 + ti, :], bn6[:, b2, :])))
                E.qsop("v", lambda v, g4=g4: v.tensor_scalar_add(
                    veps[:, g4:g4 + 4], mvall[:, g4:g4 + 4, 1:2], EPS))
                E.qsop("v", lambda v, g4=g4: v.tensor_scalar(
                    out=rstdall[:, g4:g4 + 4], in0=veps[:, g4:g4 + 4],
                    scalar1=0.5, scalar2=0.5, op0=ALU.mult, op1=ALU.add))
                E.qsop("v", lambda v, g4=g4: v.reciprocal(
                    rstdall[:, g4:g4 + 4], rstdall[:, g4:g4 + 4]))
                for _ in range(5):
                    E.qsop("v", lambda v, g4=g4: v.tensor_mul(
                        fsb[:, 0, 0:4], rstdall[:, g4:g4 + 4],
                        rstdall[:, g4:g4 + 4]))
                    E.qsop("v", lambda v, g4=g4: v.tensor_mul(
                        fsb[:, 0, 0:4], fsb[:, 0, 0:4], veps[:, g4:g4 + 4]))
                    E.qsop("v", lambda v: v.tensor_scalar(
                        out=fsb[:, 0, 0:4], in0=fsb[:, 0, 0:4],
                        scalar1=-0.5, scalar2=1.5, op0=ALU.mult, op1=ALU.add))
                    E.qsop("v", lambda v, g4=g4: v.tensor_mul(
                        rstdall[:, g4:g4 + 4], rstdall[:, g4:g4 + 4],
                        fsb[:, 0, 0:4]))
                rstd_m = E.mark("v")

                for ti in range(4):
                    tile = 4 * b + ti
                    par = tile % 2
                    if len(tr_tm) >= 2:
                        E.qwait("v", "t", tr_tm[-2])
                    v_m = E.qsop("v", dep=rstd_m, fn=lambda v,
                                 bpar=bpar, ti=ti, par=par,
                                 g4=g4: v.tensor_scalar(
                        out=xnr[:, par, :], in0=xkb4[:, bpar, ti, :],
                        scalar1=mvall[:, g4 + ti, 0:1],
                        scalar2=rstdall[:, g4 + ti:g4 + ti + 1],
                        op0=ALU.subtract, op1=ALU.mult))

                    if ti == 3:
                        nmark.append(v_m)
                    E.qwait("t", "v", v_m)
                    if len(tr_evac) >= 2:
                        E.qwait("t", "a", tr_evac[-2])
                    tr_tm.append(E.qgroup("t", [
                        (lambda t, dcc=dcc, par=par: t.transpose(
                            trA[:, par * 1024 + dcc * 128:
                                par * 1024 + (dcc + 1) * 128],
                            xnr[:, par, dcc * 128:(dcc + 1) * 128],
                            ident[:]))
                        for dcc in range(DC)], "t"))

                    E.qwait("a", "t")
                    if kv:
                        fn = (lambda a, pb=pb, ti=ti, par=par: a.activation(
                            out=xns[:, pb, :, ti * 128:(ti + 1) * 128],
                            in_=trA[:, par * 1024:(par + 1) * 1024]
                            .rearrange("p (c q) -> p c q", q=128),
                            func=AF.Copy))
                    else:
                        fn = (lambda a, tile=tile, par=par: a.activation(
                            out=xnq[:, :, tile * 128:(tile + 1) * 128],
                            in_=trA[:, par * 1024:(par + 1) * 1024]
                            .rearrange("p (c q) -> p c q", q=128),
                            func=AF.Copy))
                    tr_evac.append(E.qsop("a", fn))
                a_xns = E.mark("a")

                if kv:
                    s0 = b * 512
                    for e in range(DC):
                        ph = e % 2
                        E.qwait("t", "a", a_xns)
                        if len(k_evac) >= 2:
                            E.qwait("t", "a", k_evac[-2])
                        kv_tmark.append(E.qgroup("t", [
                            (lambda t, dcc=dcc, pb=pb, e=e, ph=ph: t.matmul(
                                psBC[:, ph * 512:(ph + 1) * 512],
                                wk_b[:, dcc, e * 128:(e + 1) * 128],
                                xns[:, pb, dcc, :],
                                start=(dcc == 0), stop=(dcc == DC - 1)))
                            for dcc in range(DC)], "t"))

                        E.qwait("a", "t", kv_tmark[-1])
                        E.qwait("a", "kst")
                        k_evac.append(E.qsop("a", lambda a, e=e, ph=ph:
                                             a.activation(
                            out=kst[:, e, :],
                            in_=psBC[:, ph * 512:(ph + 1) * 512],
                            func=AF.Copy)))
                    for sti in range(4):
                        vtgt = psD[:, 0:1024] if sti % 2 == 0 \
                            else psA[:, 1024:2048]
                        if len(v_evac) >= 2:
                            E.qwait("t", "v", v_evac[-2])
                        fns = []
                        for dcc in range(DC):
                            for eh in range(2):
                                fns.append(lambda t, dcc=dcc, eh=eh, pb=pb,
                                           sti=sti, vtgt=vtgt: t.matmul(
                                    vtgt[:, eh * 512:(eh + 1) * 512],
                                    xns[:, pb, dcc,
                                        sti * 128:(sti + 1) * 128],
                                    wv_b[:, dcc, eh * 512:(eh + 1) * 512],
                                    start=(dcc == 0), stop=(dcc == DC - 1)))
                        kv_tmark.append(E.qgroup("t", fns, "t"))

                        E.qwait("v", "t", kv_tmark[-1])
                        E.qwait("v", "vst")
                        v_evac.append(E.qsop("v", lambda v, sti=sti,
                                             vtgt=vtgt: v.tensor_copy(
                            vst[:, sti, :], vtgt)))

                    E.qwait("pl", "a")
                    E.qdma("pl", "kst", lambda g, b=b: g.dma_start(
                        out=kT_s[b],
                        in_=kst.ap().rearrange("p c q -> p (c q)")))
                    E.qwait("pl", "v")
                    E.qdma("pl", "vst", lambda g, b=b: g.dma_start(
                        out=vv_s[b],
                        in_=vst.ap().rearrange("p t e -> p (t e)")))

        xn_phase(xk_h, 64, True, 0)
        if nphase >= 3:
            xn_phase(xq_h, 16, False, 64)

    E.barrier()

    if nphase >= 4:
        q_evac = []
        q_tmark = []
        for sb in range(4):
            for e in range(DC):
                ph = e % 2
                if len(q_evac) >= 2:
                    E.qwait("t", "a", q_evac[-2])
                q_tmark.append(E.qgroup("t", [
                    (lambda t, dcc=dcc, sb=sb, e=e, ph=ph: t.matmul(
                        psBC[:, ph * 512:(ph + 1) * 512],
                        wq_b[:, dcc, e * 128:(e + 1) * 128],
                        xnq[:, dcc, sb * 512:(sb + 1) * 512],
                        start=(dcc == 0), stop=(dcc == DC - 1)))
                    for dcc in range(DC)], "t"))

                E.qwait("a", "t", q_tmark[-1])
                q_evac.append(E.qsop("a", lambda a, sb=sb, e=e, ph=ph:
                                     a.activation(
                    out=qT[:, e, sb * 512:(sb + 1) * 512],
                    in_=psBC[:, ph * 512:(ph + 1) * 512],
                    func=AF.Copy)))

    E.barrier()

    if nphase >= 5:
        E.qdma("sy", "wdT", lambda s: s.dma_start(
            out=wdT_b[:], in_=wdT_s.rearrange("(c p) e -> p c e", p=128)))
        sp_wd = E.mark("wdT")

        gu_tmark = []
        hm_vmark = []
        si_amark = []
        g_amark = []
        dn_tmark = []
        for q in range(2):
            qs = q * 1024
            for hb in range(HC):
                par = hb % 2
                if len(gu_tmark) >= 2:
                    E.qwait("sy", "t", gu_tmark[-2][1])
                E.qdma("sy", "wgu%d" % par, lambda s, hb=hb, par=par:
                       s.dma_start(
                           out=wgu[:, par, 0, :, :],
                           in_=wgT_s.rearrange("(c p) h -> p c h", p=128)
                           [:, :, hb * 128:(hb + 1) * 128]))
                E.qdma("sy", "wgu%d" % par, lambda s, hb=hb, par=par:
                       s.dma_start(
                           out=wgu[:, par, 1, :, :],
                           in_=wuT_s.rearrange("(c p) h -> p c h", p=128)
                           [:, :, hb * 128:(hb + 1) * 128]))
                sp_m = E.mark("wgu%d" % par)

                gb = (hb % 2) * 1024
                upD = hb % 2 == 1
                E.qwait("t", "wgu%d" % par, sp_m)
                if len(si_amark) >= 2:
                    E.qwait("t", "a", si_amark[-2])
                if len(hm_vmark) >= 2:
                    E.qwait("t", "v", hm_vmark[-2])
                if upD and g_amark:
                    E.qwait("t", "a", g_amark[-1])
                fns = []
                for dcc in range(DC):
                    for sh in range(2):
                        fns.append(lambda t, dcc=dcc, sh=sh, qs=qs, par=par,
                                   gb=gb: t.matmul(
                            psA[:, gb + sh * 512:gb + (sh + 1) * 512],
                            wgu[:, par, 0, dcc, :],
                            xnq[:, dcc, qs + sh * 512:qs + (sh + 1) * 512],
                            start=(dcc == 0), stop=(dcc == DC - 1)))
                m_g = E.qgroup("t", fns, "t")
                fns = []
                for dcc in range(DC):
                    for sh in range(2):
                        fns.append(lambda t, dcc=dcc, sh=sh, qs=qs, par=par,
                                   upD=upD: t.matmul(
                            (psD if upD else psBC)[:, sh * 512:
                                                   (sh + 1) * 512],
                            wgu[:, par, 1, dcc, :],
                            xnq[:, dcc, qs + sh * 512:qs + (sh + 1) * 512],
                            start=(dcc == 0), stop=(dcc == DC - 1)))
                m_u = E.qgroup("t", fns, "t")
                gu_tmark.append((m_g, m_u))

                E.qwait("a", "t", m_g)
                if hm_vmark:
                    E.qwait("a", "v", hm_vmark[-1])
                si_amark.append(E.qsop("a", lambda a, gb=gb: a.activation(
                    out=silu_t[:], in_=psA[:, gb:gb + 1024],
                    func=AF.Silu, scale=SCALE)))

                E.qwait("v", "t", m_u)
                E.qwait("v", "a", si_amark[-1])
                hm_vmark.append(E.qsop("v", lambda v, hb=hb, upD=upD:
                                       v.tensor_tensor(
                    hact[:, hb, :], silu_t[:], (psD if upD else psBC)[:],
                    ALU.mult)))
            v_hact = E.mark("v")

            for sti in range(8):
                E.qwait("t", "v", v_hact)
                E.qwait("t", "wdT", sp_wd)
                if len(g_amark) >= 1:
                    E.qwait("t", "a", g_amark[-1])
                fns = []
                for hb in range(HC):
                    for eh in range(2):
                        fns.append(lambda t, hb=hb, sti=sti, eh=eh: t.matmul(
                            psD[:, eh * 512:(eh + 1) * 512],
                            hact[:, hb, sti * 128:(sti + 1) * 128],
                            wdT_b[:, hb, eh * 512:(eh + 1) * 512],
                            start=(hb == 0), stop=(hb == HC - 1)))
                dn_tmark.append(E.qgroup("t", fns, "t"))

                E.qwait("a", "t", dn_tmark[-1])
                g_amark.append(E.qsop("a", lambda a, q=q, sti=sti:
                                      a.activation(
                    out=g_b[:, q * 8 + sti, :], in_=psD[:],
                    func=AF.Silu, scale=SCALE * SCALE)))

    E.barrier()

    if nphase >= 6:
        qk_tm = []
        av_tm = []
        ptr_tm = []
        pte_am = []
        cmb_vm = []
        kbuf_last = {}
        vbuf_last = {}
        pt_last = {}
        bank_last = {}

        def score_ap(bank):
            if bank < 4:
                return psA[:, bank * 512:(bank + 1) * 512]
            if bank < 6:
                return psBC[:, (bank - 4) * 512:(bank - 3) * 512]
            return psD[:, (bank - 6) * 512:(bank - 5) * 512]
        for pair in range(NS // 2):
            kA, kB = 2 * pair, 2 * pair + 1
            nsupA = kA + 1
            nsupB = kB + 1
            pbufs = (pbufA, pbufB)

            for sl, kk in ((0, kA), (1, kB)):
                if cmb_vm:
                    E.qwait("pl", "v", cmb_vm[-1])
                E.qdma("pl", "xrs%d" % (kk % 2), lambda g, kk=kk: g.dma_start(
                    out=xrs[:, kk % 2, :],
                    in_=xq_h[kk * 128:(kk + 1) * 128, :]))
            xr_m = (E.mark("xrs0"), E.mark("xrs1"))

            for sup in range(nsupB):
                ph = sup % 4
                kb_t = kbuf if ph < 2 else kbufX
                kb_ap = kb_t[:, ph % 2, :, :]
                pr = sup % 4
                kb_w = kbuf_last.get(ph)
                withA = sup < nsupA

                if kb_w is not None:
                    E.qwait("sy", "t", kb_w)
                E.qdma("sy", "kbuf%d" % ph, lambda s, sup=sup, kb_ap=kb_ap:
                       s.dma_start(
                           out=kb_ap,
                           in_=kT_s[sup].rearrange("p (c q) -> p c q",
                                                   q=512)))
                sp_m = E.mark("kbuf%d" % ph)

                E.qwait("t", "kbuf%d" % ph, sp_m)
                slots = ([0, 1] if withA else [1])
                for sl in slots:
                    bw = bank_last.get(pr * 2 + sl)
                    if bw is not None:
                        E.qwait("t", *bw)
                fns = []
                for sl in slots:
                    kk = kA if sl == 0 else kB
                    bk = pr * 2 + sl
                    for e in range(DC):
                        fns.append(lambda t, e=e, kk=kk, bk=bk, kb_t=kb_t,
                                   php=ph % 2: t.matmul(
                            score_ap(bk),
                            qT[:, e, kk * 128:(kk + 1) * 128],
                            kb_t[:, php, e, :],
                            start=(e == 0), stop=(e == DC - 1)))
                qk_tm.append(E.qgroup("t", fns, "t"))
                kbuf_last[ph] = qk_tm[-1]

                E.qwait("v", "t", qk_tm[-1])
                for sl in slots:
                    kk = kA if sl == 0 else kB
                    bk = pr * 2 + sl
                    if sup == (nsupA if sl == 0 else nsupB) - 1:
                        E.qsop("v", lambda v, bk=bk: v.tensor_tensor(
                            score_ap(bk), score_ap(bk),
                            maskb[:], ALU.add))
                    E.qsop("v", lambda v, bk=bk, sl=sl, sup=sup: v.reduce_max(
                        msb[:, sl, sup:sup + 1], score_ap(bk),
                        axis=AX.X, negate=True))
                    E.qsop("v", fast="msb", fn=lambda v, sl=sl, sup=sup:
                           v.tensor_scalar_mul(
                        msb[:, sl, sup:sup + 1], msb[:, sl, sup:sup + 1],
                        1.0 / 32.0))

                E.qwait("a", "v")
                for sl in slots:
                    bk = pr * 2 + sl
                    m = E.qsop("a", lambda a, sl=sl, sup=sup, bk=bk:
                               a.activation(
                        out=pbufs[sl][:, sup * 512:(sup + 1) * 512],
                        in_=score_ap(bk),
                        func=AF.Exp, bias=msb[:, sl, sup:sup + 1],
                        scale=1.0 / 32.0,
                        accum_out=lsb[:, sl, sup:sup + 1]))
                    bank_last[bk] = ("a", m)

            E.qwait("v", "a")
            for sl, nsup in ((0, nsupA), (1, nsupB)):
                E.qsop("v", fast="resc", fn=lambda v, sl=sl, nsup=nsup:
                       v.tensor_reduce(
                    mglob[:, sl:sl + 1], msb[:, sl, 0:nsup], axis=AX.X,
                    op=ALU.min))
                E.qsop("v", fast="resc", fn=lambda v, sl=sl, nsup=nsup:
                       v.tensor_single_scalar(
                    fsb[:, sl, 0:nsup], msb[:, sl, 0:nsup],
                    mglob[:, sl:sl + 1], ALU.subtract))
            E.qwait("a", "v")
            E.qsop("a", lambda a, nsupA=nsupA: a.activation(
                out=fsb[:, 0, 0:nsupA], in_=fsb[:, 0, 0:nsupA],
                func=AF.Exp, scale=-1.0))
            E.qsop("a", lambda a, nsupB=nsupB: a.activation(
                out=fsb[:, 1, 0:nsupB], in_=fsb[:, 1, 0:nsupB],
                func=AF.Exp, scale=-1.0))
            E.qwait("v", "a")
            resc_m = {}
            for sup in range(nsupB):
                for sl in ([0, 1] if sup < nsupA else [1]):
                    resc_m[(sl, sup)] = E.qsop(
                        "v", fast="resc", fn=lambda v, sl=sl, sup=sup:
                        v.tensor_scalar_mul(
                            pbufs[sl][:, sup * 512:(sup + 1) * 512],
                            pbufs[sl][:, sup * 512:(sup + 1) * 512],
                            fsb[:, sl, sup:sup + 1]))
            for sl, nsup in ((0, nsupA), (1, nsupB)):
                E.qsop("v", fast="lsb", fn=lambda v, sl=sl, nsup=nsup:
                       v.tensor_mul(
                    lsb[:, sl, 0:nsup], lsb[:, sl, 0:nsup],
                    fsb[:, sl, 0:nsup]))
                E.qsop("v", fast="lsb", fn=lambda v, sl=sl, nsup=nsup:
                       v.reduce_sum(
                    lsum[:, sl:sl + 1], lsb[:, sl, 0:nsup], axis=AX.X))
            v_resc = E.qsop("v", lambda v: v.reciprocal(linv[:], lsum[:]))

            def q_transpose(vb):
                ph = vb % 2
                tb = vb % 2
                withA = vb < nsupA
                vb_w = vbuf_last.get(ph)
                bw = bank_last.get(tb)
                pt_w = pt_last.get(ph)
                if vb_w is not None:
                    E.qwait("pl", "t", vb_w)
                E.qdma("pl", "vbuf%d" % ph, lambda g, vb=vb, ph=ph:
                       g.dma_start(
                           out=vbuf[:, ph, :, :],
                           in_=vv_s[vb].rearrange("p (t e) -> p t e",
                                                  e=DM)))
                sp_m = E.mark("vbuf%d" % ph)

                E.qwait("t", "v", resc_m[(1, vb)])
                if bw is not None:
                    E.qwait("t", *bw)
                fns = []
                for sl in ([0, 1] if withA else [1]):
                    for kt in range(4):
                        fns.append(lambda t, sl=sl, kt=kt, vb=vb, tb=tb:
                                   t.transpose(
                            trA[:, tb * 1024 + sl * 512 + kt * 128:
                                tb * 1024 + sl * 512 + (kt + 1) * 128],
                            pbufs[sl][:, vb * 512 + kt * 128:
                                      vb * 512 + (kt + 1) * 128],
                            ident[:]))
                ptr_tm.append(E.qgroup("t", fns, "t"))

                E.qwait("a", "t", ptr_tm[-1])
                if pt_w is not None:
                    E.qwait("a", "t", pt_w)
                s0 = 0 if withA else 1
                m = E.qsop("a", lambda a, ph=ph, tb=tb, s0=s0: a.activation(
                    out=pT[:, ph, s0:2, :, :],
                    in_=trA[:, tb * 1024 + s0 * 512:tb * 1024 + 1024]
                    .rearrange("p (s t q) -> p s t q", t=4, q=128),
                    func=AF.Copy))
                pte_am.append(m)
                bank_last[tb] = ("a", m)
                return (vb, ph, sp_m, m, withA)

            def q_av(stx):
                vb, ph, sp_m, pte, withA = stx
                E.qwait("t", "a", pte)
                E.qwait("t", "vbuf%d" % ph, sp_m)
                if vb == 0:
                    for bk in (4, 5, 6, 7):
                        bw = bank_last.get(bk)
                        if bw is not None:
                            E.qwait("t", *bw)
                fns = []
                for sl in ([0, 1] if withA else [1]):
                    tgt = psBC if sl == 0 else psD
                    nv = nsupA if sl == 0 else nsupB
                    for kt in range(4):
                        for eh in range(2):
                            fns.append(lambda t, sl=sl, kt=kt, eh=eh, vb=vb,
                                       ph=ph, tgt=tgt, nv=nv: t.matmul(
                                tgt[:, eh * 512:(eh + 1) * 512],
                                pT[:, ph, sl, kt, :],
                                vbuf[:, ph, kt, eh * 512:(eh + 1) * 512],
                                start=(vb == 0 and kt == 0),
                                stop=(vb == nv - 1 and kt == 3)))
                av_tm.append(E.qgroup("t", fns, "t"))
                vbuf_last[ph] = av_tm[-1]
                pt_last[ph] = av_tm[-1]

            pend = None
            for vb in range(nsupB):
                stx = q_transpose(vb)
                if pend is not None:
                    q_av(pend)
                pend = stx
            q_av(pend)

            E.qwait("v", "t", av_tm[-1])
            for sl, kk in ((0, kA), (1, kB)):
                E.qwait("v", "xrs%d" % (kk % 2), xr_m[kk % 2])
                E.qwait("v", "out%d" % (kk % 2))
                src_ps = psBC if sl == 0 else psD
                E.qsop("v", lambda v, kk=kk, sl=sl, src_ps=src_ps:
                       v.scalar_tensor_tensor(
                    out=sg_t[:], in0=src_ps[:], scalar=linv[:, sl:sl + 1],
                    in1=g_b[:, kk, :], op0=ALU.mult, op1=ALU.mult))
                m = E.qsop("v", lambda v, kk=kk: v.tensor_add(
                    outt[:, kk % 2, :], sg_t[:], xrs[:, kk % 2, :]))
                cmb_vm.append(m)
                if sl == 0:
                    bank_last[4] = bank_last[5] = ("v", m)
                else:
                    bank_last[6] = bank_last[7] = ("v", m)

                E.qwait("pl", "v", m)
                E.qdma("pl", "out%d" % (kk % 2), lambda g, kk=kk:
                       g.dma_start(
                    out=out_h[kk * 128:(kk + 1) * 128, :],
                    in_=outt[:, kk % 2, :]))

        E.qwait("pl", "out0")
        E.qwait("pl", "out1")

    with nc.Block() as blk:
        E.flush(blk)
    E.es.close()
    return nc


_NC_CACHE = {}


def _mk_inputs(X, wq, wk, wv, wg, wu, wd):
    ident = np.eye(128, dtype=np.float32).astype(BF)
    in_maps = []
    for core in range(8):
        b, j = core // 4, core % 4
        rows = np.arange(NS) * 4 + j
        xq = np.ascontiguousarray(
            X[b].reshape(64, 128, DM)[rows].reshape(QR, DM))
        c = np.arange(512)[None, :]
        r = np.arange(128)[:, None]
        mask = np.where(c <= 128 * j + r, 0.0, NEG).astype(np.float32)
        in_maps.append({
            "xk": X[b], "xq": xq,
            "wq": wq, "wk": wk, "wv": wv,
            "wg": wg, "wu": wu, "wd": wd,
            "mask": mask, "ident": ident,
        })
    return in_maps


def kernel(L0_attn_X, L0_attn_Wq, L0_attn_Wk, L0_attn_Wv,
           L0_ffn_w_gate, L0_ffn_w_up, L0_ffn_w_down):
    X = np.ascontiguousarray(np.asarray(L0_attn_X, dtype=np.float32))
    wq = np.ascontiguousarray(np.asarray(L0_attn_Wq, dtype=np.float32))
    wk = np.ascontiguousarray(np.asarray(L0_attn_Wk, dtype=np.float32))
    wv = np.ascontiguousarray(np.asarray(L0_attn_Wv, dtype=np.float32))
    wg = np.ascontiguousarray(np.asarray(L0_ffn_w_gate, dtype=np.int32))
    wu = np.ascontiguousarray(np.asarray(L0_ffn_w_up, dtype=np.int32))
    wd = np.ascontiguousarray(np.asarray(L0_ffn_w_down, dtype=np.int32))

    if "nc" not in _NC_CACHE:
        _NC_CACHE["nc"] = build()
    nc = _NC_CACHE["nc"]
    in_maps = _mk_inputs(X, wq, wk, wv, wg, wu, wd)
    res = run_bass_kernel_spmd(nc, in_maps, core_ids=list(range(8)))

    out = np.empty((2, S, DM), dtype=np.float32)
    for core in range(8):
        b, j = core // 4, core % 4
        rows = np.arange(NS) * 4 + j
        out[b].reshape(64, 128, DM)[rows] = \
            res.results[core]["out"].reshape(NS, 128, DM)
    return out


# version 22
# speedup vs baseline: 1.1096x; 1.1096x over previous
import contextlib
import numpy as np
import ml_dtypes
import concourse.bass as bass
import concourse.mybir as mybir
from concourse.bass_utils import run_bass_kernel_spmd

F32 = mybir.dt.float32
BF16 = mybir.dt.bfloat16
I32 = mybir.dt.int32
AF = mybir.ActivationFunctionType
ALU = mybir.AluOpType
AX = mybir.AxisListType
BF = ml_dtypes.bfloat16

S, DM, HID = 8192, 1024, 3072
NS = 16
QR = NS * 128
DC = DM // 128
HC = HID // 128
SCALE = 1.0 / 127.0
EPS = 1e-5
NEG = -30000.0
SELF_SYNC = set()
DBG_STATS = False
FAST_OK = set()


class Emit:

    def __init__(self, nc):
        self.nc = nc
        self.es = contextlib.ExitStack()
        self.sem = {}
        self.cnt = {}
        self.qs = {k: [] for k in ("sy", "pl", "v", "a", "t")}
        for name in ("v", "a", "t"):
            self._mk(name)

    def _mk(self, name):
        if name not in self.sem:
            self.sem[name] = self.es.enter_context(
                self.nc.semaphore("sem_" + name))
            self.cnt[name] = 0
        return self.sem[name]

    def qwait(self, ek, semkey, val=None):
        self._mk(semkey)
        v = self.cnt[semkey] if val is None else val
        self.qs[ek].append(("w", self.sem[semkey], v))

    def qop(self, ek, fn, inckey=None):
        mark = None
        if inckey is not None:
            self._mk(inckey)
            self.cnt[inckey] += 1
            mark = self.cnt[inckey]
            self.qs[ek].append(("o", fn, self.sem[inckey], 1))
        else:
            self.qs[ek].append(("o", fn, None, 0))
        return mark

    def qsop(self, ek, fn=None, fast=False, dep=None):
        key = {"v": "v", "a": "a", "t": "t"}[ek]
        if key == "v" and (fast is False or fast not in FAST_OK):
            if not (isinstance(dep, int) and dep < 0):
                self.qwait(ek, key, dep)
        return self.qop(ek, fn, key)

    def qdma(self, ek, semkey, fn):
        self._mk(semkey)
        self.cnt[semkey] += 16
        mark = self.cnt[semkey]
        self.qs[ek].append(("o", fn, self.sem[semkey], 16))
        return mark

    def qgroup(self, ek, fns, inckey):
        self._mk(inckey)
        self.cnt[inckey] += 1
        mark = self.cnt[inckey]
        self.qs[ek].append(("g", fns, self.sem[inckey], 1))
        return mark

    def mark(self, key):
        self._mk(key)
        return self.cnt[key]

    def barrier(self):
        marks = dict(self.cnt)
        for ek in self.qs:
            for kk, vv in marks.items():
                self.qs[ek].append(("w", self.sem[kk], vv))

    def flush(self, blk):
        def emit(eng, items):
            for it in items:
                if it[0] == "w":
                    eng.wait_ge(it[1], it[2])
                elif it[0] == "o":
                    ins = it[1](eng)
                    if it[2] is not None:
                        ins.then_inc(it[2], it[3])
                else:
                    for fn in it[1]:
                        ins = fn(eng)
                    ins.then_inc(it[2], it[3])

        @blk.sync
        def _(s):
            emit(s, self.qs["sy"])

        @blk.gpsimd
        def _(g):
            emit(g, self.qs["pl"])

        @blk.vector
        def _(v):
            emit(v, self.qs["v"])

        @blk.scalar
        def _(a):
            emit(a, self.qs["a"])

        @blk.tensor
        def _(t):
            emit(t, self.qs["t"])


def build(nphase=9):
    nc = bass.Bass("TRN2", target_bir_lowering=False, debug=False)

    xk_h = nc.dram_tensor("xk", [S, DM], F32, kind="ExternalInput").ap()
    xq_h = nc.dram_tensor("xq", [QR, DM], F32, kind="ExternalInput").ap()
    wq_h = nc.dram_tensor("wq", [DM, DM], F32, kind="ExternalInput").ap()
    wk_h = nc.dram_tensor("wk", [DM, DM], F32, kind="ExternalInput").ap()
    wv_h = nc.dram_tensor("wv", [DM, DM], F32, kind="ExternalInput").ap()
    wg_h = nc.dram_tensor("wg", [HID, DM], I32, kind="ExternalInput").ap()
    wu_h = nc.dram_tensor("wu", [HID, DM], I32, kind="ExternalInput").ap()
    wd_h = nc.dram_tensor("wd", [DM, HID], I32, kind="ExternalInput").ap()
    mask_h = nc.dram_tensor("mask", [128, 512], F32, kind="ExternalInput").ap()
    ident_h = nc.dram_tensor("ident", [128, 128], BF16, kind="ExternalInput").ap()
    out_h = nc.dram_tensor("out", [QR, DM], F32, kind="ExternalOutput").ap()

    kT_s = nc.dram_tensor("kT_s", [S // 512, 128, DC * 512], BF16,
                          kind="Internal").ap()
    vv_s = nc.dram_tensor("vv_s", [S // 512, 128, 4 * DM], BF16,
                          kind="Internal").ap()
    wgT_s = nc.dram_tensor("wgT_s", [DM, HID], BF16, kind="Internal").ap()
    wuT_s = nc.dram_tensor("wuT_s", [DM, HID], BF16, kind="Internal").ap()
    wdT_s = nc.dram_tensor("wdT_s", [HID, DM], BF16, kind="Internal").ap()

    BASE = 16896

    def at(name, shape, dtype, off):
        return nc.alloc_sbuf_tensor_at(name, shape, dtype, offset=BASE + off)

    wq_b = at("wq_b", [128, DC, DM], BF16, 0)
    wk_b = at("wk_b", [128, DC, DM], BF16, 16384)
    wv_b = at("wv_b", [128, DC, DM], BF16, 32768)
    xnq = at("xnq", [128, DC, QR], BF16, 49152)
    qT = at("qT", [128, DC, QR], BF16, 81920)
    g_b = at("g_b", [128, NS, DM], BF16, 114688)
    ident = at("ident_b", [128, 128], BF16, 147456)
    maskb = at("maskb", [128, 512], F32, 147712)
    bn6 = at("bn6", [128, 2, 12], F32, 149760)
    msb = at("msb", [128, 2, 16], F32, 149824)
    lsb = at("lsb", [128, 2, 16], F32, 149952)
    fsb = at("fsb", [128, 2, 16], F32, 150080)
    mglob = at("mglob", [128, 2], F32, 150208)
    lsum = at("lsum", [128, 2], F32, 150240)
    linv = at("linv", [128, 2], F32, 150272)
    veps = at("veps", [128, 80], F32, 150304)
    STG = 152064

    wst = at("wst", [128, HC, DM], BF16, 81920)
    wts = at("wts", [128, 4, 1536], BF16, 131072)
    xkb4 = at("xkb4", [128, 2, 4, DM], F32, STG)
    xnr = at("xnr", [128, 2, DM], BF16, STG + 32768)
    kst = at("kst", [128, DC, 512], BF16, STG + 36864)
    vst = at("vst", [128, 4, DM], BF16, STG + 45056)
    xns = at("xns", [128, 2, DC, 512], BF16, 49152)
    mvall = at("mvall", [128, 80, 2], F32, 150656)
    rstdall = at("rstdall", [128, 80], F32, 151296)
    wdT_b = at("wdT_b", [128, HC, DM], BF16, 0)
    hact = at("hact", [128, HC, DM], BF16, STG)
    wgu = at("wgu", [128, 2, 2, DC, 128], BF16, STG + 49152)
    silu_t = at("silu_t", [128, DM], BF16, STG + 57344)
    pbufA = at("pbufA", [128, S], BF16, 0)
    pbufB = at("pbufB", [128, S], BF16, 49152)
    kbufX = at("kbufX", [128, 2, DC, 512], BF16, 65536)
    kbuf = at("kbuf", [128, 2, DC, 512], BF16, STG)
    vbuf = at("vbuf", [128, 2, 4, DM], BF16, STG + 16384)
    pT = at("pT", [128, 2, 2, 4, 128], BF16, STG + 32768)
    outt = at("outt", [128, 2, DM], F32, STG + 36864)
    xrs = at("xrs", [128, 2, DM], F32, STG + 45056)
    sg_t = at("sg_t", [128, DM], F32, STG + 53248)

    psA = nc.alloc_psum_tensor("psA", [128, 2048], F32)
    psBC = nc.alloc_psum_tensor("psBC", [128, 1024], F32)
    psD = nc.alloc_psum_tensor("psD", [128, 1024], F32)
    trA = psA.bitcast(BF16)
    trBC = psBC.bitcast(BF16)

    E = Emit(nc)

    for w_h, w_b in ((wk_h, wk_b), (wv_h, wv_b), (wq_h, wq_b)):
        E.qdma("pl", "wqkv", lambda g, w_h=w_h, w_b=w_b: g.dma_start(
            out=w_b[:], in_=w_h.rearrange("(c p) e -> p c e", p=128)))
    E.qdma("sy", "const", lambda s: s.dma_start(out=ident[:], in_=ident_h[:]))
    E.qdma("sy", "const", lambda s: s.dma_start(out=maskb[:], in_=mask_h[:]))
    sp_const = E.mark("const")

    if nphase >= 1:
        wts_marks = []
        wts_dma = []
        wstf = wst.ap().rearrange("p a b -> p (a b)")
        tr_done = []
        for wi, (w_h, wT_s) in enumerate(
                ((wg_h, wgT_s), (wu_h, wuT_s), (wd_h, wdT_s))):
            is_d = wi == 2
            nrow = DC if is_d else HC
            ncol = HC if is_d else DC
            wfree = HID if is_d else DM
            hr = nrow // 2
            for hf in range(2):
                hpar = (wi * 2 + hf) % 2
                r_lo = hf * hr
                if len(tr_done) >= 2:
                    E.qwait("pl", "t", tr_done[-2])
                E.qdma("pl", "wst%d" % hpar, lambda g, w_h=w_h, r_lo=r_lo,
                       hr=hr, wfree=wfree, hpar=hpar: g.dma_start(
                           out=wstf[:, hpar * 12288:
                                    hpar * 12288 + hr * wfree].rearrange(
                               "p (a b) -> p a b", b=wfree),
                           in_=w_h[r_lo * 128:(r_lo + hr) * 128, :].rearrange(
                               "(c p) e -> p c e", p=128)))
                gp_m = E.mark("wst%d" % hpar)

                for cb in range(ncol):
                    nunit = (hr + 7) // 8
                    for u in range(nunit):
                        r0, r1 = u * 8, min(hr, u * 8 + 8)
                        par = (cb * nunit + u) % 4
                        E.qwait("t", "const", sp_const)
                        E.qwait("t", "wst%d" % hpar, gp_m)
                        if len(wts_marks) >= 4:
                            E.qwait("t", "a", wts_marks[-4])
                        mk = E.qgroup("t", [
                            (lambda t, rb=rb, r0=r0, cb=cb, par=par,
                             wfree=wfree, hpar=hpar: t.transpose(
                                 trA[:, par * 1024 + (rb - r0) * 128:
                                     par * 1024 + (rb - r0 + 1) * 128],
                                 wstf[:, hpar * 12288 + rb * wfree + cb * 128:
                                      hpar * 12288 + rb * wfree +
                                      (cb + 1) * 128],
                                 ident[:]))
                            for rb in range(r0, r1)], "t")

                        E.qwait("a", "t")
                        if len(wts_dma) >= 4:
                            E.qwait("a", *wts_dma[-4])
                        wts_marks.append(E.qsop("a", lambda a, r0=r0, r1=r1,
                                                par=par, wpar=cb % 4:
                                                a.activation(
                            out=wts[:, wpar, r0 * 128:r1 * 128],
                            in_=trA[:, par * 1024:
                                    par * 1024 + (r1 - r0) * 128],
                            func=AF.Copy)))

                    E.qwait("a", "a")
                    wts_dma.append(("wTs%d" % (cb % 4), E.qdma(
                        "a", "wTs%d" % (cb % 4), lambda a, cb=cb, hr=hr,
                        r_lo=r_lo, wT_s=wT_s, wpar=cb % 4: a.dma_start(
                            out=wT_s[cb * 128:(cb + 1) * 128,
                                     r_lo * 128:(r_lo + hr) * 128],
                            in_=wts[:, wpar, 0:hr * 128]))))
                tr_done.append(mk)

    if nphase >= 2:
        nq_tiles = 16 if nphase >= 3 else 0

        aggr_m = []

        nmark = []
        entries = [(xk_h, b, True) for b in range(16)]
        if nq_tiles:
            entries += [(xq_h, b, False) for b in range(4)]
        NB = len(entries)

        tr_evac = []
        tr_tm = []
        k_evac = []
        v_evac = []
        kv_tmark = []
        rstd_ms = {}
        ld_ms = {}

        def emit_load(gb):
            src_h, lb, kv = entries[gb]
            bpar = gb % 2
            if gb >= 2:
                E.qwait("sy", "v", nmark[-1])
            E.qdma("sy", "xkb4_%d" % bpar,
                   lambda s, lb=lb, bpar=bpar, src_h=src_h: s.dma_start(
                       out=xkb4[:, bpar, :, :],
                       in_=src_h[lb * 512:(lb + 1) * 512, :].rearrange(
                           "(ti p) d -> p ti d", p=128)))
            ld_ms[gb] = E.mark("xkb4_%d" % bpar)

        def emit_stats(gb):
            bpar = gb % 2
            g4 = 4 * gb
            E.qwait("v", "xkb4_%d" % bpar, ld_ms[gb])
            for ti in range(4):
                b2 = ti % 2
                E.qsop("v", lambda v, bpar=bpar, ti=ti, b2=b2: v.bn_stats(
                    bn6[:, b2, 0:6], xkb4[:, bpar, ti, 0:512]),
                    dep=(aggr_m[-2] if len(aggr_m) >= 2 else 0))
                E.qsop("v", lambda v, bpar=bpar, ti=ti, b2=b2: v.bn_stats(
                    bn6[:, b2, 6:12], xkb4[:, bpar, ti, 512:1024]),
                    dep=-10**9)
                aggr_m.append(E.qsop("v", lambda v, g4=g4, ti=ti, b2=b2:
                                     v.bn_aggr(
                    mvall[:, g4 + ti, :], bn6[:, b2, :])))
            E.qsop("v", lambda v, g4=g4: v.tensor_scalar_add(
                veps[:, g4:g4 + 4], mvall[:, g4:g4 + 4, 1:2], EPS))
            E.qsop("v", lambda v, g4=g4: v.tensor_scalar(
                out=rstdall[:, g4:g4 + 4], in0=veps[:, g4:g4 + 4],
                scalar1=0.5, scalar2=0.5, op0=ALU.mult, op1=ALU.add))
            E.qsop("v", lambda v, g4=g4: v.reciprocal(
                rstdall[:, g4:g4 + 4], rstdall[:, g4:g4 + 4]))
            for _ in range(5):
                E.qsop("v", lambda v, g4=g4: v.tensor_mul(
                    fsb[:, 0, 0:4], rstdall[:, g4:g4 + 4],
                    rstdall[:, g4:g4 + 4]))
                E.qsop("v", lambda v, g4=g4: v.tensor_mul(
                    fsb[:, 0, 0:4], fsb[:, 0, 0:4], veps[:, g4:g4 + 4]))
                E.qsop("v", lambda v: v.tensor_scalar(
                    out=fsb[:, 0, 0:4], in0=fsb[:, 0, 0:4],
                    scalar1=-0.5, scalar2=1.5, op0=ALU.mult, op1=ALU.add))
                E.qsop("v", lambda v, g4=g4: v.tensor_mul(
                    rstdall[:, g4:g4 + 4], rstdall[:, g4:g4 + 4],
                    fsb[:, 0, 0:4]))
            rstd_ms[gb] = E.mark("v")

        def emit_norms(gb):
            src_h, lb, kv = entries[gb]
            bpar = gb % 2
            pb = gb % 2
            g4 = 4 * gb
            for ti in range(4):
                tile = 4 * lb + ti
                par = ti % 2
                if len(tr_tm) >= 2:
                    E.qwait("v", "t", tr_tm[-2])
                v_m = E.qsop("v", dep=rstd_ms[gb], fn=lambda v,
                             bpar=bpar, ti=ti, par=par,
                             g4=g4: v.tensor_scalar(
                    out=xnr[:, par, :], in0=xkb4[:, bpar, ti, :],
                    scalar1=mvall[:, g4 + ti, 0:1],
                    scalar2=rstdall[:, g4 + ti:g4 + ti + 1],
                    op0=ALU.subtract, op1=ALU.mult))

                if ti == 3:
                    nmark.append(v_m)
                E.qwait("t", "v", v_m)
                if len(tr_evac) >= 2:
                    E.qwait("t", "a", tr_evac[-2])
                tr_tm.append(E.qgroup("t", [
                    (lambda t, dcc=dcc, par=par: t.transpose(
                        trA[:, par * 1024 + dcc * 128:
                            par * 1024 + (dcc + 1) * 128],
                        xnr[:, par, dcc * 128:(dcc + 1) * 128],
                        ident[:]))
                    for dcc in range(DC)], "t"))

                E.qwait("a", "t")
                if kv:
                    fn = (lambda a, pb=pb, ti=ti, par=par: a.activation(
                        out=xns[:, pb, :, ti * 128:(ti + 1) * 128],
                        in_=trA[:, par * 1024:(par + 1) * 1024]
                        .rearrange("p (c q) -> p c q", q=128),
                        func=AF.Copy))
                else:
                    fn = (lambda a, tile=tile, par=par: a.activation(
                        out=xnq[:, :, tile * 128:(tile + 1) * 128],
                        in_=trA[:, par * 1024:(par + 1) * 1024]
                        .rearrange("p (c q) -> p c q", q=128),
                        func=AF.Copy))
                tr_evac.append(E.qsop("a", fn))

        def emit_kv(gb):
            src_h, lb, kv = entries[gb]
            if not kv:
                return
            pb = gb % 2
            a_xns = E.mark("a")
            for e in range(DC):
                ph = e % 2
                E.qwait("t", "a", a_xns)
                if len(k_evac) >= 2:
                    E.qwait("t", "a", k_evac[-2])
                kv_tmark.append(E.qgroup("t", [
                    (lambda t, dcc=dcc, pb=pb, e=e, ph=ph: t.matmul(
                        psBC[:, ph * 512:(ph + 1) * 512],
                        wk_b[:, dcc, e * 128:(e + 1) * 128],
                        xns[:, pb, dcc, :],
                        start=(dcc == 0), stop=(dcc == DC - 1)))
                    for dcc in range(DC)], "t"))

                E.qwait("a", "t", kv_tmark[-1])
                E.qwait("a", "kst")
                k_evac.append(E.qsop("a", lambda a, e=e, ph=ph:
                                     a.activation(
                    out=kst[:, e, :],
                    in_=psBC[:, ph * 512:(ph + 1) * 512],
                    func=AF.Copy)))
            E.qwait("pl", "a")
            E.qdma("pl", "kst", lambda g, lb=lb: g.dma_start(
                out=kT_s[lb],
                in_=kst.ap().rearrange("p c q -> p (c q)")))
            for sti in range(4):
                vtgt = psD[:, 0:1024] if sti % 2 == 0                     else psA[:, 1024:2048]
                if len(v_evac) >= 2:
                    E.qwait("t", "a", v_evac[-2])
                fns = []
                for dcc in range(DC):
                    for eh in range(2):
                        fns.append(lambda t, dcc=dcc, eh=eh, pb=pb,
                                   sti=sti, vtgt=vtgt: t.matmul(
                            vtgt[:, eh * 512:(eh + 1) * 512],
                            xns[:, pb, dcc,
                                sti * 128:(sti + 1) * 128],
                            wv_b[:, dcc, eh * 512:(eh + 1) * 512],
                            start=(dcc == 0), stop=(dcc == DC - 1)))
                kv_tmark.append(E.qgroup("t", fns, "t"))

                E.qwait("a", "t", kv_tmark[-1])
                E.qwait("a", "vst")
                v_evac.append(E.qsop("a", lambda a, sti=sti,
                                     vtgt=vtgt: a.activation(
                    out=vst[:, sti, :], in_=vtgt, func=AF.Copy)))

            E.qwait("pl", "a")
            E.qdma("pl", "vst", lambda g, lb=lb: g.dma_start(
                out=vv_s[lb],
                in_=vst.ap().rearrange("p t e -> p (t e)")))

        E.qwait("t", "a")
        E.qwait("t", "wqkv")
        emit_load(0)
        emit_stats(0)
        emit_load(1)
        for gb in range(NB):
            emit_norms(gb)
            if gb + 2 < NB:
                emit_load(gb + 2)
            if gb + 1 < NB:
                emit_stats(gb + 1)
            emit_kv(gb)

    E.barrier()

    if nphase >= 4:
        q_evac = []
        q_tmark = []
        for sb in range(4):
            for e in range(DC):
                ph = e % 2
                if len(q_evac) >= 2:
                    E.qwait("t", "a", q_evac[-2])
                q_tmark.append(E.qgroup("t", [
                    (lambda t, dcc=dcc, sb=sb, e=e, ph=ph: t.matmul(
                        psBC[:, ph * 512:(ph + 1) * 512],
                        wq_b[:, dcc, e * 128:(e + 1) * 128],
                        xnq[:, dcc, sb * 512:(sb + 1) * 512],
                        start=(dcc == 0), stop=(dcc == DC - 1)))
                    for dcc in range(DC)], "t"))

                E.qwait("a", "t", q_tmark[-1])
                q_evac.append(E.qsop("a", lambda a, sb=sb, e=e, ph=ph:
                                     a.activation(
                    out=qT[:, e, sb * 512:(sb + 1) * 512],
                    in_=psBC[:, ph * 512:(ph + 1) * 512],
                    func=AF.Copy)))

    E.barrier()

    if nphase >= 5:
        E.qdma("sy", "wdT", lambda s: s.dma_start(
            out=wdT_b[:], in_=wdT_s.rearrange("(c p) e -> p c e", p=128)))
        sp_wd = E.mark("wdT")

        gu_tmark = []
        hm_vmark = []
        si_amark = []
        g_amark = []
        dn_tmark = []
        for q in range(2):
            qs = q * 1024
            for hb in range(HC):
                par = hb % 2
                if len(gu_tmark) >= 2:
                    E.qwait("sy", "t", gu_tmark[-2][1])
                E.qdma("sy", "wgu%d" % par, lambda s, hb=hb, par=par:
                       s.dma_start(
                           out=wgu[:, par, 0, :, :],
                           in_=wgT_s.rearrange("(c p) h -> p c h", p=128)
                           [:, :, hb * 128:(hb + 1) * 128]))
                E.qdma("sy", "wgu%d" % par, lambda s, hb=hb, par=par:
                       s.dma_start(
                           out=wgu[:, par, 1, :, :],
                           in_=wuT_s.rearrange("(c p) h -> p c h", p=128)
                           [:, :, hb * 128:(hb + 1) * 128]))
                sp_m = E.mark("wgu%d" % par)

                gb = (hb % 2) * 1024
                upD = hb % 2 == 1
                E.qwait("t", "wgu%d" % par, sp_m)
                if len(si_amark) >= 2:
                    E.qwait("t", "a", si_amark[-2])
                if len(hm_vmark) >= 2:
                    E.qwait("t", "v", hm_vmark[-2])
                if upD and g_amark:
                    E.qwait("t", "a", g_amark[-1])
                fns = []
                for dcc in range(DC):
                    for sh in range(2):
                        fns.append(lambda t, dcc=dcc, sh=sh, qs=qs, par=par,
                                   gb=gb: t.matmul(
                            psA[:, gb + sh * 512:gb + (sh + 1) * 512],
                            wgu[:, par, 0, dcc, :],
                            xnq[:, dcc, qs + sh * 512:qs + (sh + 1) * 512],
                            start=(dcc == 0), stop=(dcc == DC - 1)))
                m_g = E.qgroup("t", fns, "t")
                fns = []
                for dcc in range(DC):
                    for sh in range(2):
                        fns.append(lambda t, dcc=dcc, sh=sh, qs=qs, par=par,
                                   upD=upD: t.matmul(
                            (psD if upD else psBC)[:, sh * 512:
                                                   (sh + 1) * 512],
                            wgu[:, par, 1, dcc, :],
                            xnq[:, dcc, qs + sh * 512:qs + (sh + 1) * 512],
                            start=(dcc == 0), stop=(dcc == DC - 1)))
                m_u = E.qgroup("t", fns, "t")
                gu_tmark.append((m_g, m_u))

                E.qwait("a", "t", m_g)
                if hm_vmark:
                    E.qwait("a", "v", hm_vmark[-1])
                si_amark.append(E.qsop("a", lambda a, gb=gb: a.activation(
                    out=silu_t[:], in_=psA[:, gb:gb + 1024],
                    func=AF.Silu, scale=SCALE)))

                E.qwait("v", "t", m_u)
                E.qwait("v", "a", si_amark[-1])
                hm_vmark.append(E.qsop("v", lambda v, hb=hb, upD=upD:
                                       v.tensor_tensor(
                    hact[:, hb, :], silu_t[:], (psD if upD else psBC)[:],
                    ALU.mult)))
            v_hact = E.mark("v")

            for sti in range(8):
                E.qwait("t", "v", v_hact)
                E.qwait("t", "wdT", sp_wd)
                if len(g_amark) >= 1:
                    E.qwait("t", "a", g_amark[-1])
                fns = []
                for hb in range(HC):
                    for eh in range(2):
                        fns.append(lambda t, hb=hb, sti=sti, eh=eh: t.matmul(
                            psD[:, eh * 512:(eh + 1) * 512],
                            hact[:, hb, sti * 128:(sti + 1) * 128],
                            wdT_b[:, hb, eh * 512:(eh + 1) * 512],
                            start=(hb == 0), stop=(hb == HC - 1)))
                dn_tmark.append(E.qgroup("t", fns, "t"))

                E.qwait("a", "t", dn_tmark[-1])
                g_amark.append(E.qsop("a", lambda a, q=q, sti=sti:
                                      a.activation(
                    out=g_b[:, q * 8 + sti, :], in_=psD[:],
                    func=AF.Silu, scale=SCALE * SCALE)))

    E.barrier()

    if nphase >= 6:
        qk_tm = []
        av_tm = []
        ptr_tm = []
        pte_am = []
        cmb_vm = []
        kbuf_last = {}
        vbuf_last = {}
        pt_last = {}
        bank_last = {}

        def score_ap(bank):
            if bank < 4:
                return psA[:, bank * 512:(bank + 1) * 512]
            if bank < 6:
                return psBC[:, (bank - 4) * 512:(bank - 3) * 512]
            return psD[:, (bank - 6) * 512:(bank - 5) * 512]
        for pair in range(NS // 2):
            kA, kB = 2 * pair, 2 * pair + 1
            nsupA = kA + 1
            nsupB = kB + 1
            pbufs = (pbufA, pbufB)

            for sl, kk in ((0, kA), (1, kB)):
                if cmb_vm:
                    E.qwait("pl", "v", cmb_vm[-1])
                E.qdma("pl", "xrs%d" % (kk % 2), lambda g, kk=kk: g.dma_start(
                    out=xrs[:, kk % 2, :],
                    in_=xq_h[kk * 128:(kk + 1) * 128, :]))
            xr_m = (E.mark("xrs0"), E.mark("xrs1"))

            for sup in range(nsupB):
                ph = sup % 4
                kb_t = kbuf if ph < 2 else kbufX
                kb_ap = kb_t[:, ph % 2, :, :]
                pr = sup % 4
                kb_w = kbuf_last.get(ph)
                withA = sup < nsupA

                if kb_w is not None:
                    E.qwait("sy", "t", kb_w)
                E.qdma("sy", "kbuf%d" % ph, lambda s, sup=sup, kb_ap=kb_ap:
                       s.dma_start(
                           out=kb_ap,
                           in_=kT_s[sup].rearrange("p (c q) -> p c q",
                                                   q=512)))
                sp_m = E.mark("kbuf%d" % ph)

                E.qwait("t", "kbuf%d" % ph, sp_m)
                slots = ([0, 1] if withA else [1])
                for sl in slots:
                    bw = bank_last.get(pr * 2 + sl)
                    if bw is not None:
                        E.qwait("t", *bw)
                fns = []
                for sl in slots:
                    kk = kA if sl == 0 else kB
                    bk = pr * 2 + sl
                    for e in range(DC):
                        fns.append(lambda t, e=e, kk=kk, bk=bk, kb_t=kb_t,
                                   php=ph % 2: t.matmul(
                            score_ap(bk),
                            qT[:, e, kk * 128:(kk + 1) * 128],
                            kb_t[:, php, e, :],
                            start=(e == 0), stop=(e == DC - 1)))
                qk_tm.append(E.qgroup("t", fns, "t"))
                kbuf_last[ph] = qk_tm[-1]

                E.qwait("v", "t", qk_tm[-1])
                for sl in slots:
                    kk = kA if sl == 0 else kB
                    bk = pr * 2 + sl
                    if sup == (nsupA if sl == 0 else nsupB) - 1:
                        E.qsop("v", lambda v, bk=bk: v.tensor_tensor(
                            score_ap(bk), score_ap(bk),
                            maskb[:], ALU.add))
                    E.qsop("v", lambda v, bk=bk, sl=sl, sup=sup: v.reduce_max(
                        msb[:, sl, sup:sup + 1], score_ap(bk),
                        axis=AX.X, negate=True))
                    E.qsop("v", fast="msb", fn=lambda v, sl=sl, sup=sup:
                           v.tensor_scalar_mul(
                        msb[:, sl, sup:sup + 1], msb[:, sl, sup:sup + 1],
                        1.0 / 32.0))

                E.qwait("a", "v")
                for sl in slots:
                    bk = pr * 2 + sl
                    m = E.qsop("a", lambda a, sl=sl, sup=sup, bk=bk:
                               a.activation(
                        out=pbufs[sl][:, sup * 512:(sup + 1) * 512],
                        in_=score_ap(bk),
                        func=AF.Exp, bias=msb[:, sl, sup:sup + 1],
                        scale=1.0 / 32.0,
                        accum_out=lsb[:, sl, sup:sup + 1]))
                    bank_last[bk] = ("a", m)

            E.qwait("v", "a")
            for sl, nsup in ((0, nsupA), (1, nsupB)):
                E.qsop("v", fast="resc", fn=lambda v, sl=sl, nsup=nsup:
                       v.tensor_reduce(
                    mglob[:, sl:sl + 1], msb[:, sl, 0:nsup], axis=AX.X,
                    op=ALU.min))
                E.qsop("v", fast="resc", fn=lambda v, sl=sl, nsup=nsup:
                       v.tensor_single_scalar(
                    fsb[:, sl, 0:nsup], msb[:, sl, 0:nsup],
                    mglob[:, sl:sl + 1], ALU.subtract))
            E.qwait("a", "v")
            E.qsop("a", lambda a, nsupA=nsupA: a.activation(
                out=fsb[:, 0, 0:nsupA], in_=fsb[:, 0, 0:nsupA],
                func=AF.Exp, scale=-1.0))
            E.qsop("a", lambda a, nsupB=nsupB: a.activation(
                out=fsb[:, 1, 0:nsupB], in_=fsb[:, 1, 0:nsupB],
                func=AF.Exp, scale=-1.0))
            E.qwait("v", "a")
            resc_m = {}
            for sup in range(nsupB):
                for sl in ([0, 1] if sup < nsupA else [1]):
                    resc_m[(sl, sup)] = E.qsop(
                        "v", fast="resc", fn=lambda v, sl=sl, sup=sup:
                        v.tensor_scalar_mul(
                            pbufs[sl][:, sup * 512:(sup + 1) * 512],
                            pbufs[sl][:, sup * 512:(sup + 1) * 512],
                            fsb[:, sl, sup:sup + 1]))
            for sl, nsup in ((0, nsupA), (1, nsupB)):
                E.qsop("v", fast="lsb", fn=lambda v, sl=sl, nsup=nsup:
                       v.tensor_mul(
                    lsb[:, sl, 0:nsup], lsb[:, sl, 0:nsup],
                    fsb[:, sl, 0:nsup]))
                E.qsop("v", fast="lsb", fn=lambda v, sl=sl, nsup=nsup:
                       v.reduce_sum(
                    lsum[:, sl:sl + 1], lsb[:, sl, 0:nsup], axis=AX.X))
            v_resc = E.qsop("v", lambda v: v.reciprocal(linv[:], lsum[:]))

            def q_transpose(vb):
                ph = vb % 2
                tb = vb % 2
                withA = vb < nsupA
                vb_w = vbuf_last.get(ph)
                bw = bank_last.get(tb)
                pt_w = pt_last.get(ph)
                if vb_w is not None:
                    E.qwait("pl", "t", vb_w)
                E.qdma("pl", "vbuf%d" % ph, lambda g, vb=vb, ph=ph:
                       g.dma_start(
                           out=vbuf[:, ph, :, :],
                           in_=vv_s[vb].rearrange("p (t e) -> p t e",
                                                  e=DM)))
                sp_m = E.mark("vbuf%d" % ph)

                E.qwait("t", "v", resc_m[(1, vb)])
                if bw is not None:
                    E.qwait("t", *bw)
                fns = []
                for sl in ([0, 1] if withA else [1]):
                    for kt in range(4):
                        fns.append(lambda t, sl=sl, kt=kt, vb=vb, tb=tb:
                                   t.transpose(
                            trA[:, tb * 1024 + sl * 512 + kt * 128:
                                tb * 1024 + sl * 512 + (kt + 1) * 128],
                            pbufs[sl][:, vb * 512 + kt * 128:
                                      vb * 512 + (kt + 1) * 128],
                            ident[:]))
                ptr_tm.append(E.qgroup("t", fns, "t"))

                E.qwait("a", "t", ptr_tm[-1])
                if pt_w is not None:
                    E.qwait("a", "t", pt_w)
                s0 = 0 if withA else 1
                m = E.qsop("a", lambda a, ph=ph, tb=tb, s0=s0: a.activation(
                    out=pT[:, ph, s0:2, :, :],
                    in_=trA[:, tb * 1024 + s0 * 512:tb * 1024 + 1024]
                    .rearrange("p (s t q) -> p s t q", t=4, q=128),
                    func=AF.Copy))
                pte_am.append(m)
                bank_last[tb] = ("a", m)
                return (vb, ph, sp_m, m, withA)

            def q_av(stx):
                vb, ph, sp_m, pte, withA = stx
                E.qwait("t", "a", pte)
                E.qwait("t", "vbuf%d" % ph, sp_m)
                if vb == 0:
                    for bk in (4, 5, 6, 7):
                        bw = bank_last.get(bk)
                        if bw is not None:
                            E.qwait("t", *bw)
                fns = []
                for sl in ([0, 1] if withA else [1]):
                    tgt = psBC if sl == 0 else psD
                    nv = nsupA if sl == 0 else nsupB
                    for kt in range(4):
                        for eh in range(2):
                            fns.append(lambda t, sl=sl, kt=kt, eh=eh, vb=vb,
                                       ph=ph, tgt=tgt, nv=nv: t.matmul(
                                tgt[:, eh * 512:(eh + 1) * 512],
                                pT[:, ph, sl, kt, :],
                                vbuf[:, ph, kt, eh * 512:(eh + 1) * 512],
                                start=(vb == 0 and kt == 0),
                                stop=(vb == nv - 1 and kt == 3)))
                av_tm.append(E.qgroup("t", fns, "t"))
                vbuf_last[ph] = av_tm[-1]
                pt_last[ph] = av_tm[-1]

            pend = None
            for vb in range(nsupB):
                stx = q_transpose(vb)
                if pend is not None:
                    q_av(pend)
                pend = stx
            q_av(pend)

            E.qwait("v", "t", av_tm[-1])
            for sl, kk in ((0, kA), (1, kB)):
                E.qwait("v", "xrs%d" % (kk % 2), xr_m[kk % 2])
                E.qwait("v", "out%d" % (kk % 2))
                src_ps = psBC if sl == 0 else psD
                E.qsop("v", lambda v, kk=kk, sl=sl, src_ps=src_ps:
                       v.scalar_tensor_tensor(
                    out=sg_t[:], in0=src_ps[:], scalar=linv[:, sl:sl + 1],
                    in1=g_b[:, kk, :], op0=ALU.mult, op1=ALU.mult))
                m = E.qsop("v", lambda v, kk=kk: v.tensor_add(
                    outt[:, kk % 2, :], sg_t[:], xrs[:, kk % 2, :]))
                cmb_vm.append(m)
                if sl == 0:
                    bank_last[4] = bank_last[5] = ("v", m)
                else:
                    bank_last[6] = bank_last[7] = ("v", m)

                E.qwait("pl", "v", m)
                E.qdma("pl", "out%d" % (kk % 2), lambda g, kk=kk:
                       g.dma_start(
                    out=out_h[kk * 128:(kk + 1) * 128, :],
                    in_=outt[:, kk % 2, :]))

        E.qwait("pl", "out0")
        E.qwait("pl", "out1")

    with nc.Block() as blk:
        E.flush(blk)
    E.es.close()
    return nc


_NC_CACHE = {}


def _mk_inputs(X, wq, wk, wv, wg, wu, wd):
    ident = np.eye(128, dtype=np.float32).astype(BF)
    in_maps = []
    for core in range(8):
        b, j = core // 4, core % 4
        rows = np.arange(NS) * 4 + j
        xq = np.ascontiguousarray(
            X[b].reshape(64, 128, DM)[rows].reshape(QR, DM))
        c = np.arange(512)[None, :]
        r = np.arange(128)[:, None]
        mask = np.where(c <= 128 * j + r, 0.0, NEG).astype(np.float32)
        in_maps.append({
            "xk": X[b], "xq": xq,
            "wq": wq, "wk": wk, "wv": wv,
            "wg": wg, "wu": wu, "wd": wd,
            "mask": mask, "ident": ident,
        })
    return in_maps


def kernel(L0_attn_X, L0_attn_Wq, L0_attn_Wk, L0_attn_Wv,
           L0_ffn_w_gate, L0_ffn_w_up, L0_ffn_w_down):
    X = np.ascontiguousarray(np.asarray(L0_attn_X, dtype=np.float32))
    wq = np.ascontiguousarray(np.asarray(L0_attn_Wq, dtype=np.float32))
    wk = np.ascontiguousarray(np.asarray(L0_attn_Wk, dtype=np.float32))
    wv = np.ascontiguousarray(np.asarray(L0_attn_Wv, dtype=np.float32))
    wg = np.ascontiguousarray(np.asarray(L0_ffn_w_gate, dtype=np.int32))
    wu = np.ascontiguousarray(np.asarray(L0_ffn_w_up, dtype=np.int32))
    wd = np.ascontiguousarray(np.asarray(L0_ffn_w_down, dtype=np.int32))

    if "nc" not in _NC_CACHE:
        _NC_CACHE["nc"] = build()
    nc = _NC_CACHE["nc"]
    in_maps = _mk_inputs(X, wq, wk, wv, wg, wu, wd)
    res = run_bass_kernel_spmd(nc, in_maps, core_ids=list(range(8)))

    out = np.empty((2, S, DM), dtype=np.float32)
    for core in range(8):
        b, j = core // 4, core % 4
        rows = np.arange(NS) * 4 + j
        out[b].reshape(64, 128, DM)[rows] = \
            res.results[core]["out"].reshape(NS, 128, DM)
    return out


# version 23
# speedup vs baseline: 1.1574x; 1.0431x over previous
import contextlib
import numpy as np
import ml_dtypes
import concourse.bass as bass
import concourse.mybir as mybir
from concourse.bass_utils import run_bass_kernel_spmd

F32 = mybir.dt.float32
BF16 = mybir.dt.bfloat16
I32 = mybir.dt.int32
AF = mybir.ActivationFunctionType
ALU = mybir.AluOpType
AX = mybir.AxisListType
BF = ml_dtypes.bfloat16

S, DM, HID = 8192, 1024, 3072
NS = 16
QR = NS * 128
DC = DM // 128
HC = HID // 128
SCALE = 1.0 / 127.0
EPS = 1e-5
NEG = -30000.0
SELF_SYNC = set()
DBG_STATS = False
FAST_OK = set()


class Emit:

    def __init__(self, nc):
        self.nc = nc
        self.es = contextlib.ExitStack()
        self.sem = {}
        self.cnt = {}
        self.qs = {k: [] for k in ("sy", "pl", "v", "a", "t")}
        for name in ("v", "a", "t"):
            self._mk(name)

    def _mk(self, name):
        if name not in self.sem:
            self.sem[name] = self.es.enter_context(
                self.nc.semaphore("sem_" + name))
            self.cnt[name] = 0
        return self.sem[name]

    def qwait(self, ek, semkey, val=None):
        self._mk(semkey)
        v = self.cnt[semkey] if val is None else val
        self.qs[ek].append(("w", self.sem[semkey], v))

    def qop(self, ek, fn, inckey=None):
        mark = None
        if inckey is not None:
            self._mk(inckey)
            self.cnt[inckey] += 1
            mark = self.cnt[inckey]
            self.qs[ek].append(("o", fn, self.sem[inckey], 1))
        else:
            self.qs[ek].append(("o", fn, None, 0))
        return mark

    def qsop(self, ek, fn=None, fast=False, dep=None):
        key = {"v": "v", "a": "a", "t": "t"}[ek]
        if key == "v" and (fast is False or fast not in FAST_OK):
            if not (isinstance(dep, int) and dep < 0):
                self.qwait(ek, key, dep)
        return self.qop(ek, fn, key)

    def qdma(self, ek, semkey, fn):
        self._mk(semkey)
        self.cnt[semkey] += 16
        mark = self.cnt[semkey]
        self.qs[ek].append(("o", fn, self.sem[semkey], 16))
        return mark

    def qgroup(self, ek, fns, inckey):
        self._mk(inckey)
        self.cnt[inckey] += 1
        mark = self.cnt[inckey]
        self.qs[ek].append(("g", fns, self.sem[inckey], 1))
        return mark

    def mark(self, key):
        self._mk(key)
        return self.cnt[key]

    def barrier(self):
        marks = dict(self.cnt)
        for ek in self.qs:
            for kk, vv in marks.items():
                self.qs[ek].append(("w", self.sem[kk], vv))

    def flush(self, blk):
        def emit(eng, items):
            for it in items:
                if it[0] == "w":
                    eng.wait_ge(it[1], it[2])
                elif it[0] == "o":
                    ins = it[1](eng)
                    if it[2] is not None:
                        ins.then_inc(it[2], it[3])
                else:
                    for fn in it[1]:
                        ins = fn(eng)
                    ins.then_inc(it[2], it[3])

        @blk.sync
        def _(s):
            emit(s, self.qs["sy"])

        @blk.gpsimd
        def _(g):
            emit(g, self.qs["pl"])

        @blk.vector
        def _(v):
            emit(v, self.qs["v"])

        @blk.scalar
        def _(a):
            emit(a, self.qs["a"])

        @blk.tensor
        def _(t):
            emit(t, self.qs["t"])


def build(nphase=9):
    nc = bass.Bass("TRN2", target_bir_lowering=False, debug=False)

    xk_h = nc.dram_tensor("xk", [S, DM], F32, kind="ExternalInput").ap()
    xq_h = nc.dram_tensor("xq", [QR, DM], F32, kind="ExternalInput").ap()
    wq_h = nc.dram_tensor("wq", [DM, DM], F32, kind="ExternalInput").ap()
    wk_h = nc.dram_tensor("wk", [DM, DM], F32, kind="ExternalInput").ap()
    wv_h = nc.dram_tensor("wv", [DM, DM], F32, kind="ExternalInput").ap()
    wg_h = nc.dram_tensor("wg", [HID, DM], I32, kind="ExternalInput").ap()
    wu_h = nc.dram_tensor("wu", [HID, DM], I32, kind="ExternalInput").ap()
    wd_h = nc.dram_tensor("wd", [DM, HID], I32, kind="ExternalInput").ap()
    mask_h = nc.dram_tensor("mask", [128, 512], F32, kind="ExternalInput").ap()
    ident_h = nc.dram_tensor("ident", [128, 128], BF16, kind="ExternalInput").ap()
    out_h = nc.dram_tensor("out", [QR, DM], F32, kind="ExternalOutput").ap()

    kT_s = nc.dram_tensor("kT_s", [S // 512, 128, DC * 512], BF16,
                          kind="Internal").ap()
    vv_s = nc.dram_tensor("vv_s", [S // 512, 128, 4 * DM], BF16,
                          kind="Internal").ap()
    wgT_s = nc.dram_tensor("wgT_s", [DM, HID], BF16, kind="Internal").ap()
    wuT_s = nc.dram_tensor("wuT_s", [DM, HID], BF16, kind="Internal").ap()
    wdT_s = nc.dram_tensor("wdT_s", [HID, DM], BF16, kind="Internal").ap()

    BASE = 16896

    def at(name, shape, dtype, off):
        return nc.alloc_sbuf_tensor_at(name, shape, dtype, offset=BASE + off)

    wq_b = at("wq_b", [128, DC, DM], BF16, 0)
    wk_b = at("wk_b", [128, DC, DM], BF16, 16384)
    wv_b = at("wv_b", [128, DC, DM], BF16, 32768)
    xnq = at("xnq", [128, DC, QR], BF16, 49152)
    qT = at("qT", [128, DC, QR], BF16, 81920)
    g_b = at("g_b", [128, NS, DM], BF16, 114688)
    ident = at("ident_b", [128, 128], BF16, 147456)
    maskb = at("maskb", [128, 512], F32, 147712)
    bn6 = at("bn6", [128, 2, 12], F32, 149760)
    msb = at("msb", [128, 2, 16], F32, 149824)
    lsb = at("lsb", [128, 2, 16], F32, 149952)
    fsb = at("fsb", [128, 2, 16], F32, 150080)
    mglob = at("mglob", [128, 2], F32, 150208)
    lsum = at("lsum", [128, 2], F32, 150240)
    linv = at("linv", [128, 2], F32, 150272)
    veps = at("veps", [128, 80], F32, 150304)
    STG = 152064

    wst = at("wst", [128, HC, DM], BF16, 81920)
    wts = at("wts", [128, 4, 1536], BF16, 131072)
    xkb4 = at("xkb4", [128, 2, 4, DM], F32, STG)
    xnr = at("xnr", [128, 2, DM], BF16, STG + 32768)
    kst = at("kst", [128, DC, 512], BF16, STG + 36864)
    vst = at("vst", [128, 4, DM], BF16, STG + 45056)
    xns = at("xns", [128, 2, DC, 512], BF16, 49152)
    mvall = at("mvall", [128, 80, 2], F32, 150656)
    rstdall = at("rstdall", [128, 80], F32, 151296)
    wdT_b = at("wdT_b", [128, HC, DM], BF16, 0)
    hact = at("hact", [128, HC, DM], BF16, STG)
    wgu = at("wgu", [128, 2, 2, DC, 128], BF16, STG + 49152)
    silu_t = at("silu_t", [128, DM], BF16, STG + 57344)
    pbufA = at("pbufA", [128, S], BF16, 0)
    pbufB = at("pbufB", [128, S], BF16, 49152)
    kbufX = at("kbufX", [128, 2, DC, 512], BF16, 65536)
    kbuf = at("kbuf", [128, 2, DC, 512], BF16, STG)
    vbuf = at("vbuf", [128, 2, 4, DM], BF16, STG + 16384)
    pT = at("pT", [128, 2, 2, 4, 128], BF16, STG + 32768)
    outt = at("outt", [128, 2, DM], F32, STG + 36864)
    xrs = at("xrs", [128, 2, DM], F32, STG + 45056)
    sg_t = at("sg_t", [128, DM], F32, STG + 53248)

    psA = nc.alloc_psum_tensor("psA", [128, 2048], F32)
    psBC = nc.alloc_psum_tensor("psBC", [128, 1024], F32)
    psD = nc.alloc_psum_tensor("psD", [128, 1024], F32)
    trA = psA.bitcast(BF16)
    trBC = psBC.bitcast(BF16)

    E = Emit(nc)

    for w_h, w_b in ((wk_h, wk_b), (wv_h, wv_b), (wq_h, wq_b)):
        E.qdma("pl", "wqkv", lambda g, w_h=w_h, w_b=w_b: g.dma_start(
            out=w_b[:], in_=w_h.rearrange("(c p) e -> p c e", p=128)))
    E.qdma("sy", "const", lambda s: s.dma_start(out=ident[:], in_=ident_h[:]))
    E.qdma("sy", "const", lambda s: s.dma_start(out=maskb[:], in_=mask_h[:]))
    sp_const = E.mark("const")

    if nphase >= 1:
        wts_marks = []
        wts_dma = []
        wstf = wst.ap().rearrange("p a b -> p (a b)")
        tr_done = []
        for wi, (w_h, wT_s) in enumerate(
                ((wg_h, wgT_s), (wu_h, wuT_s), (wd_h, wdT_s))):
            is_d = wi == 2
            nrow = DC if is_d else HC
            ncol = HC if is_d else DC
            wfree = HID if is_d else DM
            hr = nrow // 2
            for hf in range(2):
                hpar = (wi * 2 + hf) % 2
                r_lo = hf * hr
                if len(tr_done) >= 2:
                    E.qwait("pl", "t", tr_done[-2])
                E.qdma("pl", "wst%d" % hpar, lambda g, w_h=w_h, r_lo=r_lo,
                       hr=hr, wfree=wfree, hpar=hpar: g.dma_start(
                           out=wstf[:, hpar * 12288:
                                    hpar * 12288 + hr * wfree].rearrange(
                               "p (a b) -> p a b", b=wfree),
                           in_=w_h[r_lo * 128:(r_lo + hr) * 128, :].rearrange(
                               "(c p) e -> p c e", p=128)))
                gp_m = E.mark("wst%d" % hpar)

                for cb in range(ncol):
                    nunit = (hr + 7) // 8
                    for u in range(nunit):
                        r0, r1 = u * 8, min(hr, u * 8 + 8)
                        par = (cb * nunit + u) % 4
                        E.qwait("t", "const", sp_const)
                        E.qwait("t", "wst%d" % hpar, gp_m)
                        if len(wts_marks) >= 4:
                            E.qwait("t", "a", wts_marks[-4])
                        mk = E.qgroup("t", [
                            (lambda t, rb=rb, r0=r0, cb=cb, par=par,
                             wfree=wfree, hpar=hpar: t.transpose(
                                 trA[:, par * 1024 + (rb - r0) * 128:
                                     par * 1024 + (rb - r0 + 1) * 128],
                                 wstf[:, hpar * 12288 + rb * wfree + cb * 128:
                                      hpar * 12288 + rb * wfree +
                                      (cb + 1) * 128],
                                 ident[:]))
                            for rb in range(r0, r1)], "t")

                        E.qwait("a", "t")
                        if len(wts_dma) >= 4:
                            E.qwait("a", *wts_dma[-4])
                        wts_marks.append(E.qsop("a", lambda a, r0=r0, r1=r1,
                                                par=par, wpar=cb % 4:
                                                a.activation(
                            out=wts[:, wpar, r0 * 128:r1 * 128],
                            in_=trA[:, par * 1024:
                                    par * 1024 + (r1 - r0) * 128],
                            func=AF.Copy)))

                    E.qwait("a", "a")
                    wts_dma.append(("wTs%d" % (cb % 4), E.qdma(
                        "a", "wTs%d" % (cb % 4), lambda a, cb=cb, hr=hr,
                        r_lo=r_lo, wT_s=wT_s, wpar=cb % 4: a.dma_start(
                            out=wT_s[cb * 128:(cb + 1) * 128,
                                     r_lo * 128:(r_lo + hr) * 128],
                            in_=wts[:, wpar, 0:hr * 128]))))
                tr_done.append(mk)

    if nphase >= 2:
        nq_tiles = 16 if nphase >= 3 else 0

        aggr_m = []

        nmark = []
        entries = [(xk_h, b, True) for b in range(16)]
        if nq_tiles:
            entries += [(xq_h, b, False) for b in range(4)]
        NB = len(entries)

        tr_evac = []
        tr_tm = []
        k_evac = []
        v_evac = []
        kv_tmark = []
        rstd_ms = {}
        ld_ms = {}

        def emit_load(gb):
            src_h, lb, kv = entries[gb]
            bpar = gb % 2
            if gb >= 2:
                E.qwait("sy", "v", nmark[-1])
            E.qdma("sy", "xkb4_%d" % bpar,
                   lambda s, lb=lb, bpar=bpar, src_h=src_h: s.dma_start(
                       out=xkb4[:, bpar, :, :],
                       in_=src_h[lb * 512:(lb + 1) * 512, :].rearrange(
                           "(ti p) d -> p ti d", p=128)))
            ld_ms[gb] = E.mark("xkb4_%d" % bpar)

        def emit_stats(gb):
            bpar = gb % 2
            g4 = 4 * gb
            E.qwait("v", "xkb4_%d" % bpar, ld_ms[gb])
            for ti in range(4):
                b2 = ti % 2
                E.qsop("v", lambda v, bpar=bpar, ti=ti, b2=b2: v.bn_stats(
                    bn6[:, b2, 0:6], xkb4[:, bpar, ti, 0:512]),
                    dep=(aggr_m[-2] if len(aggr_m) >= 2 else 0))
                E.qsop("v", lambda v, bpar=bpar, ti=ti, b2=b2: v.bn_stats(
                    bn6[:, b2, 6:12], xkb4[:, bpar, ti, 512:1024]),
                    dep=-10**9)
                aggr_m.append(E.qsop("v", lambda v, g4=g4, ti=ti, b2=b2:
                                     v.bn_aggr(
                    mvall[:, g4 + ti, :], bn6[:, b2, :])))
            E.qsop("v", lambda v, g4=g4: v.tensor_scalar_add(
                veps[:, g4:g4 + 4], mvall[:, g4:g4 + 4, 1:2], EPS))
            E.qsop("v", lambda v, g4=g4: v.tensor_scalar(
                out=rstdall[:, g4:g4 + 4], in0=veps[:, g4:g4 + 4],
                scalar1=0.5, scalar2=0.5, op0=ALU.mult, op1=ALU.add))
            E.qsop("v", lambda v, g4=g4: v.reciprocal(
                rstdall[:, g4:g4 + 4], rstdall[:, g4:g4 + 4]))
            for _ in range(5):
                E.qsop("v", lambda v, g4=g4: v.tensor_mul(
                    fsb[:, 0, 0:4], rstdall[:, g4:g4 + 4],
                    rstdall[:, g4:g4 + 4]))
                E.qsop("v", lambda v, g4=g4: v.tensor_mul(
                    fsb[:, 0, 0:4], fsb[:, 0, 0:4], veps[:, g4:g4 + 4]))
                E.qsop("v", lambda v: v.tensor_scalar(
                    out=fsb[:, 0, 0:4], in0=fsb[:, 0, 0:4],
                    scalar1=-0.5, scalar2=1.5, op0=ALU.mult, op1=ALU.add))
                E.qsop("v", lambda v, g4=g4: v.tensor_mul(
                    rstdall[:, g4:g4 + 4], rstdall[:, g4:g4 + 4],
                    fsb[:, 0, 0:4]))
            rstd_ms[gb] = E.mark("v")

        xnsm = {}

        def emit_ladder(gb):
            src_h, lb, kv = entries[gb]
            bpar = gb % 2
            pb = gb % 2
            g4 = 4 * gb
            for ti in range(4):
                tile = 4 * lb + ti
                par = ti % 2
                if len(tr_tm) >= 2:
                    E.qwait("v", "t", tr_tm[-2])
                v_m = E.qsop("v", dep=rstd_ms[gb], fn=lambda v,
                             bpar=bpar, ti=ti, par=par,
                             g4=g4: v.tensor_scalar(
                    out=xnr[:, par, :], in0=xkb4[:, bpar, ti, :],
                    scalar1=mvall[:, g4 + ti, 0:1],
                    scalar2=rstdall[:, g4 + ti:g4 + ti + 1],
                    op0=ALU.subtract, op1=ALU.mult))

                if ti == 3:
                    nmark.append(v_m)
                E.qwait("t", "v", v_m)
                if len(tr_evac) >= 2:
                    E.qwait("t", "a", tr_evac[-2])
                tr_tm.append(E.qgroup("t", [
                    (lambda t, dcc=dcc, par=par: t.transpose(
                        trA[:, par * 1024 + dcc * 128:
                            par * 1024 + (dcc + 1) * 128],
                        xnr[:, par, dcc * 128:(dcc + 1) * 128],
                        ident[:]))
                    for dcc in range(DC)], "t"))

                E.qwait("a", "t")
                if kv:
                    fn = (lambda a, pb=pb, ti=ti, par=par: a.activation(
                        out=xns[:, pb, :, ti * 128:(ti + 1) * 128],
                        in_=trA[:, par * 1024:(par + 1) * 1024]
                        .rearrange("p (c q) -> p c q", q=128),
                        func=AF.Copy))
                else:
                    fn = (lambda a, tile=tile, par=par: a.activation(
                        out=xnq[:, :, tile * 128:(tile + 1) * 128],
                        in_=trA[:, par * 1024:(par + 1) * 1024]
                        .rearrange("p (c q) -> p c q", q=128),
                        func=AF.Copy))
                tr_evac.append(E.qsop("a", fn))
            xnsm[gb] = E.mark("a")

        def emit_K(gb):
            src_h, lb, kv = entries[gb]
            pb = gb % 2
            a_xns = xnsm[gb]
            for e in range(DC):
                ph = e % 2
                E.qwait("t", "a", a_xns)
                if len(k_evac) >= 2:
                    E.qwait("t", "a", k_evac[-2])
                kv_tmark.append(E.qgroup("t", [
                    (lambda t, dcc=dcc, pb=pb, e=e, ph=ph: t.matmul(
                        psBC[:, ph * 512:(ph + 1) * 512],
                        wk_b[:, dcc, e * 128:(e + 1) * 128],
                        xns[:, pb, dcc, :],
                        start=(dcc == 0), stop=(dcc == DC - 1)))
                    for dcc in range(DC)], "t"))

                E.qwait("a", "t", kv_tmark[-1])
                E.qwait("a", "kst")
                k_evac.append(E.qsop("a", lambda a, e=e, ph=ph:
                                     a.activation(
                    out=kst[:, e, :],
                    in_=psBC[:, ph * 512:(ph + 1) * 512],
                    func=AF.Copy)))
            E.qwait("pl", "a")
            E.qdma("pl", "kst", lambda g, lb=lb: g.dma_start(
                out=kT_s[lb],
                in_=kst.ap().rearrange("p c q -> p (c q)")))

        def emit_V(gb):
            src_h, lb, kv = entries[gb]
            pb = gb % 2
            for sti in range(4):
                vtgt = psD[:, 0:1024] if sti % 2 == 0                     else psA[:, 1024:2048]
                if len(v_evac) >= 2:
                    E.qwait("t", "a", v_evac[-2])
                fns = []
                for dcc in range(DC):
                    for eh in range(2):
                        fns.append(lambda t, dcc=dcc, eh=eh, pb=pb,
                                   sti=sti, vtgt=vtgt: t.matmul(
                            vtgt[:, eh * 512:(eh + 1) * 512],
                            xns[:, pb, dcc,
                                sti * 128:(sti + 1) * 128],
                            wv_b[:, dcc, eh * 512:(eh + 1) * 512],
                            start=(dcc == 0), stop=(dcc == DC - 1)))
                kv_tmark.append(E.qgroup("t", fns, "t"))

                E.qwait("a", "t", kv_tmark[-1])
                E.qwait("a", "vst")
                v_evac.append(E.qsop("a", lambda a, sti=sti,
                                     vtgt=vtgt: a.activation(
                    out=vst[:, sti, :], in_=vtgt, func=AF.Copy)))

            E.qwait("pl", "a")
            E.qdma("pl", "vst", lambda g, lb=lb: g.dma_start(
                out=vv_s[lb],
                in_=vst.ap().rearrange("p t e -> p (t e)")))

        def emit_Q(sb):
            for e in range(DC):
                ph = e % 2
                E.qwait("t", "a", xnsm[16 + sb])
                if len(k_evac) >= 2:
                    E.qwait("t", "a", k_evac[-2])
                kv_tmark.append(E.qgroup("t", [
                    (lambda t, dcc=dcc, sb=sb, e=e, ph=ph: t.matmul(
                        psBC[:, ph * 512:(ph + 1) * 512],
                        wq_b[:, dcc, e * 128:(e + 1) * 128],
                        xnq[:, dcc, sb * 512:(sb + 1) * 512],
                        start=(dcc == 0), stop=(dcc == DC - 1)))
                    for dcc in range(DC)], "t"))
                E.qwait("a", "t", kv_tmark[-1])
                k_evac.append(E.qsop("a", lambda a, sb=sb, e=e, ph=ph:
                                     a.activation(
                    out=qT[:, e, sb * 512:(sb + 1) * 512],
                    in_=psBC[:, ph * 512:(ph + 1) * 512],
                    func=AF.Copy)))

        E.qwait("t", "a")
        E.qwait("t", "wqkv")
        emit_load(0)
        emit_stats(0)
        emit_load(1)
        emit_stats(1)
        emit_ladder(0)
        emit_load(2)
        for gb in range(NB):
            kv = entries[gb][2]
            boundary = kv and gb + 1 < NB and not entries[gb + 1][2]
            if kv:
                emit_K(gb)
            elif nphase >= 4:
                emit_Q(gb - 16)
            if not boundary:
                if gb + 1 < NB:
                    emit_ladder(gb + 1)
                if gb + 3 < NB:
                    emit_load(gb + 3)
                if gb + 2 < NB:
                    emit_stats(gb + 2)
            if kv:
                emit_V(gb)
            if boundary:
                emit_ladder(gb + 1)
                if gb + 3 < NB:
                    emit_load(gb + 3)
                if gb + 2 < NB:
                    emit_stats(gb + 2)

    E.barrier()

    E.barrier()

    if nphase >= 5:
        E.qdma("sy", "wdT", lambda s: s.dma_start(
            out=wdT_b[:], in_=wdT_s.rearrange("(c p) e -> p c e", p=128)))
        sp_wd = E.mark("wdT")

        gu_tmark = []
        hm_vmark = []
        si_amark = []
        g_amark = []
        dn_tmark = []
        for q in range(2):
            qs = q * 1024
            for hb in range(HC):
                par = hb % 2
                if len(gu_tmark) >= 2:
                    E.qwait("sy", "t", gu_tmark[-2][1])
                E.qdma("sy", "wgu%d" % par, lambda s, hb=hb, par=par:
                       s.dma_start(
                           out=wgu[:, par, 0, :, :],
                           in_=wgT_s.rearrange("(c p) h -> p c h", p=128)
                           [:, :, hb * 128:(hb + 1) * 128]))
                E.qdma("sy", "wgu%d" % par, lambda s, hb=hb, par=par:
                       s.dma_start(
                           out=wgu[:, par, 1, :, :],
                           in_=wuT_s.rearrange("(c p) h -> p c h", p=128)
                           [:, :, hb * 128:(hb + 1) * 128]))
                sp_m = E.mark("wgu%d" % par)

                gb = (hb % 2) * 1024
                upD = hb % 2 == 1
                E.qwait("t", "wgu%d" % par, sp_m)
                if len(si_amark) >= 2:
                    E.qwait("t", "a", si_amark[-2])
                if len(hm_vmark) >= 2:
                    E.qwait("t", "v", hm_vmark[-2])
                if upD and g_amark:
                    E.qwait("t", "a", g_amark[-1])
                fns = []
                for dcc in range(DC):
                    for sh in range(2):
                        fns.append(lambda t, dcc=dcc, sh=sh, qs=qs, par=par,
                                   gb=gb: t.matmul(
                            psA[:, gb + sh * 512:gb + (sh + 1) * 512],
                            wgu[:, par, 0, dcc, :],
                            xnq[:, dcc, qs + sh * 512:qs + (sh + 1) * 512],
                            start=(dcc == 0), stop=(dcc == DC - 1)))
                m_g = E.qgroup("t", fns, "t")
                fns = []
                for dcc in range(DC):
                    for sh in range(2):
                        fns.append(lambda t, dcc=dcc, sh=sh, qs=qs, par=par,
                                   upD=upD: t.matmul(
                            (psD if upD else psBC)[:, sh * 512:
                                                   (sh + 1) * 512],
                            wgu[:, par, 1, dcc, :],
                            xnq[:, dcc, qs + sh * 512:qs + (sh + 1) * 512],
                            start=(dcc == 0), stop=(dcc == DC - 1)))
                m_u = E.qgroup("t", fns, "t")
                gu_tmark.append((m_g, m_u))

                E.qwait("a", "t", m_g)
                if hm_vmark:
                    E.qwait("a", "v", hm_vmark[-1])
                si_amark.append(E.qsop("a", lambda a, gb=gb: a.activation(
                    out=silu_t[:], in_=psA[:, gb:gb + 1024],
                    func=AF.Silu, scale=SCALE)))

                E.qwait("v", "t", m_u)
                E.qwait("v", "a", si_amark[-1])
                hm_vmark.append(E.qsop("v", lambda v, hb=hb, upD=upD:
                                       v.tensor_tensor(
                    hact[:, hb, :], silu_t[:], (psD if upD else psBC)[:],
                    ALU.mult)))
            v_hact = E.mark("v")

            for sti in range(8):
                E.qwait("t", "v", v_hact)
                E.qwait("t", "wdT", sp_wd)
                if len(g_amark) >= 1:
                    E.qwait("t", "a", g_amark[-1])
                fns = []
                for hb in range(HC):
                    for eh in range(2):
                        fns.append(lambda t, hb=hb, sti=sti, eh=eh: t.matmul(
                            psD[:, eh * 512:(eh + 1) * 512],
                            hact[:, hb, sti * 128:(sti + 1) * 128],
                            wdT_b[:, hb, eh * 512:(eh + 1) * 512],
                            start=(hb == 0), stop=(hb == HC - 1)))
                dn_tmark.append(E.qgroup("t", fns, "t"))

                E.qwait("a", "t", dn_tmark[-1])
                g_amark.append(E.qsop("a", lambda a, q=q, sti=sti:
                                      a.activation(
                    out=g_b[:, q * 8 + sti, :], in_=psD[:],
                    func=AF.Silu, scale=SCALE * SCALE)))

    E.barrier()

    if nphase >= 6:
        qk_tm = []
        av_tm = []
        ptr_tm = []
        pte_am = []
        cmb_vm = []
        kbuf_last = {}
        vbuf_last = {}
        pt_last = {}
        bank_last = {}

        def score_ap(bank):
            if bank < 4:
                return psA[:, bank * 512:(bank + 1) * 512]
            if bank < 6:
                return psBC[:, (bank - 4) * 512:(bank - 3) * 512]
            return psD[:, (bank - 6) * 512:(bank - 5) * 512]
        for pair in range(NS // 2):
            kA, kB = 2 * pair, 2 * pair + 1
            nsupA = kA + 1
            nsupB = kB + 1
            pbufs = (pbufA, pbufB)

            for sl, kk in ((0, kA), (1, kB)):
                if cmb_vm:
                    E.qwait("pl", "v", cmb_vm[-1])
                E.qdma("pl", "xrs%d" % (kk % 2), lambda g, kk=kk: g.dma_start(
                    out=xrs[:, kk % 2, :],
                    in_=xq_h[kk * 128:(kk + 1) * 128, :]))
            xr_m = (E.mark("xrs0"), E.mark("xrs1"))

            for sup in range(nsupB):
                ph = sup % 4
                kb_t = kbuf if ph < 2 else kbufX
                kb_ap = kb_t[:, ph % 2, :, :]
                pr = sup % 4
                kb_w = kbuf_last.get(ph)
                withA = sup < nsupA

                if kb_w is not None:
                    E.qwait("sy", "t", kb_w)
                E.qdma("sy", "kbuf%d" % ph, lambda s, sup=sup, kb_ap=kb_ap:
                       s.dma_start(
                           out=kb_ap,
                           in_=kT_s[sup].rearrange("p (c q) -> p c q",
                                                   q=512)))
                sp_m = E.mark("kbuf%d" % ph)

                E.qwait("t", "kbuf%d" % ph, sp_m)
                slots = ([0, 1] if withA else [1])
                for sl in slots:
                    bw = bank_last.get(pr * 2 + sl)
                    if bw is not None:
                        E.qwait("t", *bw)
                fns = []
                for sl in slots:
                    kk = kA if sl == 0 else kB
                    bk = pr * 2 + sl
                    for e in range(DC):
                        fns.append(lambda t, e=e, kk=kk, bk=bk, kb_t=kb_t,
                                   php=ph % 2: t.matmul(
                            score_ap(bk),
                            qT[:, e, kk * 128:(kk + 1) * 128],
                            kb_t[:, php, e, :],
                            start=(e == 0), stop=(e == DC - 1)))
                qk_tm.append(E.qgroup("t", fns, "t"))
                kbuf_last[ph] = qk_tm[-1]

                E.qwait("v", "t", qk_tm[-1])
                for sl in slots:
                    kk = kA if sl == 0 else kB
                    bk = pr * 2 + sl
                    if sup == (nsupA if sl == 0 else nsupB) - 1:
                        E.qsop("v", lambda v, bk=bk: v.tensor_tensor(
                            score_ap(bk), score_ap(bk),
                            maskb[:], ALU.add))
                    E.qsop("v", lambda v, bk=bk, sl=sl, sup=sup: v.reduce_max(
                        msb[:, sl, sup:sup + 1], score_ap(bk),
                        axis=AX.X, negate=True))
                    E.qsop("v", fast="msb", fn=lambda v, sl=sl, sup=sup:
                           v.tensor_scalar_mul(
                        msb[:, sl, sup:sup + 1], msb[:, sl, sup:sup + 1],
                        1.0 / 32.0))

                E.qwait("a", "v")
                for sl in slots:
                    bk = pr * 2 + sl
                    m = E.qsop("a", lambda a, sl=sl, sup=sup, bk=bk:
                               a.activation(
                        out=pbufs[sl][:, sup * 512:(sup + 1) * 512],
                        in_=score_ap(bk),
                        func=AF.Exp, bias=msb[:, sl, sup:sup + 1],
                        scale=1.0 / 32.0,
                        accum_out=lsb[:, sl, sup:sup + 1]))
                    bank_last[bk] = ("a", m)

            E.qwait("v", "a")
            for sl, nsup in ((0, nsupA), (1, nsupB)):
                E.qsop("v", fast="resc", fn=lambda v, sl=sl, nsup=nsup:
                       v.tensor_reduce(
                    mglob[:, sl:sl + 1], msb[:, sl, 0:nsup], axis=AX.X,
                    op=ALU.min))
                E.qsop("v", fast="resc", fn=lambda v, sl=sl, nsup=nsup:
                       v.tensor_single_scalar(
                    fsb[:, sl, 0:nsup], msb[:, sl, 0:nsup],
                    mglob[:, sl:sl + 1], ALU.subtract))
            E.qwait("a", "v")
            E.qsop("a", lambda a, nsupA=nsupA: a.activation(
                out=fsb[:, 0, 0:nsupA], in_=fsb[:, 0, 0:nsupA],
                func=AF.Exp, scale=-1.0))
            E.qsop("a", lambda a, nsupB=nsupB: a.activation(
                out=fsb[:, 1, 0:nsupB], in_=fsb[:, 1, 0:nsupB],
                func=AF.Exp, scale=-1.0))
            E.qwait("v", "a")
            resc_m = {}
            for sup in range(nsupB):
                for sl in ([0, 1] if sup < nsupA else [1]):
                    resc_m[(sl, sup)] = E.qsop(
                        "v", fast="resc", fn=lambda v, sl=sl, sup=sup:
                        v.tensor_scalar_mul(
                            pbufs[sl][:, sup * 512:(sup + 1) * 512],
                            pbufs[sl][:, sup * 512:(sup + 1) * 512],
                            fsb[:, sl, sup:sup + 1]))
            for sl, nsup in ((0, nsupA), (1, nsupB)):
                E.qsop("v", fast="lsb", fn=lambda v, sl=sl, nsup=nsup:
                       v.tensor_mul(
                    lsb[:, sl, 0:nsup], lsb[:, sl, 0:nsup],
                    fsb[:, sl, 0:nsup]))
                E.qsop("v", fast="lsb", fn=lambda v, sl=sl, nsup=nsup:
                       v.reduce_sum(
                    lsum[:, sl:sl + 1], lsb[:, sl, 0:nsup], axis=AX.X))
            v_resc = E.qsop("v", lambda v: v.reciprocal(linv[:], lsum[:]))

            def q_transpose(vb):
                ph = vb % 2
                tb = vb % 2
                withA = vb < nsupA
                vb_w = vbuf_last.get(ph)
                bw = bank_last.get(tb)
                pt_w = pt_last.get(ph)
                if vb_w is not None:
                    E.qwait("pl", "t", vb_w)
                E.qdma("pl", "vbuf%d" % ph, lambda g, vb=vb, ph=ph:
                       g.dma_start(
                           out=vbuf[:, ph, :, :],
                           in_=vv_s[vb].rearrange("p (t e) -> p t e",
                                                  e=DM)))
                sp_m = E.mark("vbuf%d" % ph)

                E.qwait("t", "v", resc_m[(1, vb)])
                if bw is not None:
                    E.qwait("t", *bw)
                fns = []
                for sl in ([0, 1] if withA else [1]):
                    for kt in range(4):
                        fns.append(lambda t, sl=sl, kt=kt, vb=vb, tb=tb:
                                   t.transpose(
                            trA[:, tb * 1024 + sl * 512 + kt * 128:
                                tb * 1024 + sl * 512 + (kt + 1) * 128],
                            pbufs[sl][:, vb * 512 + kt * 128:
                                      vb * 512 + (kt + 1) * 128],
                            ident[:]))
                ptr_tm.append(E.qgroup("t", fns, "t"))

                E.qwait("a", "t", ptr_tm[-1])
                if pt_w is not None:
                    E.qwait("a", "t", pt_w)
                s0 = 0 if withA else 1
                m = E.qsop("a", lambda a, ph=ph, tb=tb, s0=s0: a.activation(
                    out=pT[:, ph, s0:2, :, :],
                    in_=trA[:, tb * 1024 + s0 * 512:tb * 1024 + 1024]
                    .rearrange("p (s t q) -> p s t q", t=4, q=128),
                    func=AF.Copy))
                pte_am.append(m)
                bank_last[tb] = ("a", m)
                return (vb, ph, sp_m, m, withA)

            def q_av(stx):
                vb, ph, sp_m, pte, withA = stx
                E.qwait("t", "a", pte)
                E.qwait("t", "vbuf%d" % ph, sp_m)
                if vb == 0:
                    for bk in (4, 5, 6, 7):
                        bw = bank_last.get(bk)
                        if bw is not None:
                            E.qwait("t", *bw)
                fns = []
                for sl in ([0, 1] if withA else [1]):
                    tgt = psBC if sl == 0 else psD
                    nv = nsupA if sl == 0 else nsupB
                    for kt in range(4):
                        for eh in range(2):
                            fns.append(lambda t, sl=sl, kt=kt, eh=eh, vb=vb,
                                       ph=ph, tgt=tgt, nv=nv: t.matmul(
                                tgt[:, eh * 512:(eh + 1) * 512],
                                pT[:, ph, sl, kt, :],
                                vbuf[:, ph, kt, eh * 512:(eh + 1) * 512],
                                start=(vb == 0 and kt == 0),
                                stop=(vb == nv - 1 and kt == 3)))
                av_tm.append(E.qgroup("t", fns, "t"))
                vbuf_last[ph] = av_tm[-1]
                pt_last[ph] = av_tm[-1]

            pend = None
            for vb in range(nsupB):
                stx = q_transpose(vb)
                if pend is not None:
                    q_av(pend)
                pend = stx
            q_av(pend)

            E.qwait("v", "t", av_tm[-1])
            for sl, kk in ((0, kA), (1, kB)):
                E.qwait("v", "xrs%d" % (kk % 2), xr_m[kk % 2])
                E.qwait("v", "out%d" % (kk % 2))
                src_ps = psBC if sl == 0 else psD
                E.qsop("v", lambda v, kk=kk, sl=sl, src_ps=src_ps:
                       v.scalar_tensor_tensor(
                    out=sg_t[:], in0=src_ps[:], scalar=linv[:, sl:sl + 1],
                    in1=g_b[:, kk, :], op0=ALU.mult, op1=ALU.mult))
                m = E.qsop("v", lambda v, kk=kk: v.tensor_add(
                    outt[:, kk % 2, :], sg_t[:], xrs[:, kk % 2, :]))
                cmb_vm.append(m)
                if sl == 0:
                    bank_last[4] = bank_last[5] = ("v", m)
                else:
                    bank_last[6] = bank_last[7] = ("v", m)

                E.qwait("pl", "v", m)
                E.qdma("pl", "out%d" % (kk % 2), lambda g, kk=kk:
                       g.dma_start(
                    out=out_h[kk * 128:(kk + 1) * 128, :],
                    in_=outt[:, kk % 2, :]))

        E.qwait("pl", "out0")
        E.qwait("pl", "out1")

    with nc.Block() as blk:
        E.flush(blk)
    E.es.close()
    return nc


_NC_CACHE = {}


def _mk_inputs(X, wq, wk, wv, wg, wu, wd):
    ident = np.eye(128, dtype=np.float32).astype(BF)
    in_maps = []
    for core in range(8):
        b, j = core // 4, core % 4
        rows = np.arange(NS) * 4 + j
        xq = np.ascontiguousarray(
            X[b].reshape(64, 128, DM)[rows].reshape(QR, DM))
        c = np.arange(512)[None, :]
        r = np.arange(128)[:, None]
        mask = np.where(c <= 128 * j + r, 0.0, NEG).astype(np.float32)
        in_maps.append({
            "xk": X[b], "xq": xq,
            "wq": wq, "wk": wk, "wv": wv,
            "wg": wg, "wu": wu, "wd": wd,
            "mask": mask, "ident": ident,
        })
    return in_maps


def kernel(L0_attn_X, L0_attn_Wq, L0_attn_Wk, L0_attn_Wv,
           L0_ffn_w_gate, L0_ffn_w_up, L0_ffn_w_down):
    X = np.ascontiguousarray(np.asarray(L0_attn_X, dtype=np.float32))
    wq = np.ascontiguousarray(np.asarray(L0_attn_Wq, dtype=np.float32))
    wk = np.ascontiguousarray(np.asarray(L0_attn_Wk, dtype=np.float32))
    wv = np.ascontiguousarray(np.asarray(L0_attn_Wv, dtype=np.float32))
    wg = np.ascontiguousarray(np.asarray(L0_ffn_w_gate, dtype=np.int32))
    wu = np.ascontiguousarray(np.asarray(L0_ffn_w_up, dtype=np.int32))
    wd = np.ascontiguousarray(np.asarray(L0_ffn_w_down, dtype=np.int32))

    if "nc" not in _NC_CACHE:
        _NC_CACHE["nc"] = build()
    nc = _NC_CACHE["nc"]
    in_maps = _mk_inputs(X, wq, wk, wv, wg, wu, wd)
    res = run_bass_kernel_spmd(nc, in_maps, core_ids=list(range(8)))

    out = np.empty((2, S, DM), dtype=np.float32)
    for core in range(8):
        b, j = core // 4, core % 4
        rows = np.arange(NS) * 4 + j
        out[b].reshape(64, 128, DM)[rows] = \
            res.results[core]["out"].reshape(NS, 128, DM)
    return out


# version 25
# speedup vs baseline: 1.2218x; 1.0556x over previous
import contextlib
import numpy as np
import ml_dtypes
import concourse.bass as bass
import concourse.mybir as mybir
from concourse.bass_utils import run_bass_kernel_spmd

F32 = mybir.dt.float32
BF16 = mybir.dt.bfloat16
I32 = mybir.dt.int32
AF = mybir.ActivationFunctionType
ALU = mybir.AluOpType
AX = mybir.AxisListType
BF = ml_dtypes.bfloat16

S, DM, HID = 8192, 1024, 3072
NS = 16
QR = NS * 128
DC = DM // 128
HC = HID // 128
SCALE = 1.0 / 127.0
EPS = 1e-5
NEG = -30000.0
SELF_SYNC = set()
DBG_STATS = False
FAST_OK = set()


class Emit:

    def __init__(self, nc):
        self.nc = nc
        self.es = contextlib.ExitStack()
        self.sem = {}
        self.cnt = {}
        self.qs = {k: [] for k in ("sy", "pl", "v", "a", "t")}
        for name in ("v", "a", "t"):
            self._mk(name)

    def _mk(self, name):
        if name not in self.sem:
            self.sem[name] = self.es.enter_context(
                self.nc.semaphore("sem_" + name))
            self.cnt[name] = 0
        return self.sem[name]

    def qwait(self, ek, semkey, val=None):
        self._mk(semkey)
        v = self.cnt[semkey] if val is None else val
        self.qs[ek].append(("w", self.sem[semkey], v))

    def qop(self, ek, fn, inckey=None):
        mark = None
        if inckey is not None:
            self._mk(inckey)
            self.cnt[inckey] += 1
            mark = self.cnt[inckey]
            self.qs[ek].append(("o", fn, self.sem[inckey], 1))
        else:
            self.qs[ek].append(("o", fn, None, 0))
        return mark

    def qsop(self, ek, fn=None, fast=False, dep=None):
        key = {"v": "v", "a": "a", "t": "t"}[ek]
        if key == "v" and (fast is False or fast not in FAST_OK):
            if not (isinstance(dep, int) and dep < 0):
                self.qwait(ek, key, dep)
        return self.qop(ek, fn, key)

    def qdma(self, ek, semkey, fn):
        self._mk(semkey)
        self.cnt[semkey] += 16
        mark = self.cnt[semkey]
        self.qs[ek].append(("o", fn, self.sem[semkey], 16))
        return mark

    def qgroup(self, ek, fns, inckey):
        self._mk(inckey)
        self.cnt[inckey] += 1
        mark = self.cnt[inckey]
        self.qs[ek].append(("g", fns, self.sem[inckey], 1))
        return mark

    def mark(self, key):
        self._mk(key)
        return self.cnt[key]

    def barrier(self):
        marks = dict(self.cnt)
        for ek in self.qs:
            for kk, vv in marks.items():
                self.qs[ek].append(("w", self.sem[kk], vv))

    def flush(self, blk):
        def emit(eng, items):
            for it in items:
                if it[0] == "w":
                    eng.wait_ge(it[1], it[2])
                elif it[0] == "o":
                    ins = it[1](eng)
                    if it[2] is not None:
                        ins.then_inc(it[2], it[3])
                else:
                    for fn in it[1]:
                        ins = fn(eng)
                    ins.then_inc(it[2], it[3])

        @blk.sync
        def _(s):
            emit(s, self.qs["sy"])

        @blk.gpsimd
        def _(g):
            emit(g, self.qs["pl"])

        @blk.vector
        def _(v):
            emit(v, self.qs["v"])

        @blk.scalar
        def _(a):
            emit(a, self.qs["a"])

        @blk.tensor
        def _(t):
            emit(t, self.qs["t"])


def build(nphase=9):
    nc = bass.Bass("TRN2", target_bir_lowering=False, debug=False)

    xk_h = nc.dram_tensor("xk", [S, DM], F32, kind="ExternalInput").ap()
    xq_h = nc.dram_tensor("xq", [QR, DM], F32, kind="ExternalInput").ap()
    wq_h = nc.dram_tensor("wq", [DM, DM], F32, kind="ExternalInput").ap()
    wk_h = nc.dram_tensor("wk", [DM, DM], F32, kind="ExternalInput").ap()
    wv_h = nc.dram_tensor("wv", [DM, DM], F32, kind="ExternalInput").ap()
    wg_h = nc.dram_tensor("wg", [HID, DM], I32, kind="ExternalInput").ap()
    wu_h = nc.dram_tensor("wu", [HID, DM], I32, kind="ExternalInput").ap()
    wd_h = nc.dram_tensor("wd", [DM, HID], I32, kind="ExternalInput").ap()
    mask_h = nc.dram_tensor("mask", [128, 512], F32, kind="ExternalInput").ap()
    ident_h = nc.dram_tensor("ident", [128, 128], BF16, kind="ExternalInput").ap()
    out_h = nc.dram_tensor("out", [QR, DM], F32, kind="ExternalOutput").ap()

    kT_s = nc.dram_tensor("kT_s", [S // 512, 128, DC * 512], BF16,
                          kind="Internal").ap()
    vv_s = nc.dram_tensor("vv_s", [S // 512, 128, 4 * DM], BF16,
                          kind="Internal").ap()
    wgT_s = nc.dram_tensor("wgT_s", [DM, HID], BF16, kind="Internal").ap()
    wuT_s = nc.dram_tensor("wuT_s", [DM, HID], BF16, kind="Internal").ap()
    wdT_s = nc.dram_tensor("wdT_s", [HID, DM], BF16, kind="Internal").ap()

    BASE = 16896

    def at(name, shape, dtype, off):
        return nc.alloc_sbuf_tensor_at(name, shape, dtype, offset=BASE + off)

    wq_b = at("wq_b", [128, DC, DM], BF16, 0)
    wk_b = at("wk_b", [128, DC, DM], BF16, 16384)
    wv_b = at("wv_b", [128, DC, DM], BF16, 32768)
    xnq = at("xnq", [128, DC, QR], BF16, 49152)
    qT = at("qT", [128, DC, QR], BF16, 81920)
    g_b = at("g_b", [128, NS, DM], BF16, 114688)
    ident = at("ident_b", [128, 128], BF16, 147456)
    maskb = at("maskb", [128, 512], F32, 147712)
    bn6 = at("bn6", [128, 2, 12], F32, 149760)
    msb = at("msb", [128, 2, 16], F32, 149824)
    lsb = at("lsb", [128, 2, 16], F32, 149952)
    fsb = at("fsb", [128, 2, 16], F32, 150080)
    mglob = at("mglob", [128, 2], F32, 150208)
    lsum = at("lsum", [128, 2], F32, 150240)
    linv = at("linv", [128, 2], F32, 150272)
    veps = at("veps", [128, 80], F32, 150304)
    STG = 152064

    wst = at("wst", [128, HC, DM], BF16, 81920)
    wts = at("wts", [128, 4, 1536], BF16, 131072)
    xkb4 = at("xkb4", [128, 2, 4, DM], F32, STG)
    xnr = at("xnr", [128, 2, DM], BF16, STG + 32768)
    kst = at("kst", [128, DC, 512], BF16, STG + 36864)
    vst = at("vst", [128, 4, DM], BF16, STG + 45056)
    xns = at("xns", [128, 2, DC, 512], BF16, 49152)
    mvall = at("mvall", [128, 80, 2], F32, 150656)
    rstdall = at("rstdall", [128, 80], F32, 151296)
    wdT_b = at("wdT_b", [128, HC, DM], BF16, 0)
    hact = at("hact", [128, HC, DM], BF16, STG)
    wgu = at("wgu", [128, 2, 2, DC, 128], BF16, STG + 49152)
    silu_t = at("silu_t", [128, DM], BF16, STG + 57344)
    pbufA = at("pbufA", [128, S], BF16, 0)
    pbufB = at("pbufB", [128, S], BF16, 49152)
    kbufX = at("kbufX", [128, 2, DC, 512], BF16, 65536)
    kbuf = at("kbuf", [128, 2, DC, 512], BF16, STG)
    vbuf = at("vbuf", [128, 2, 4, DM], BF16, STG + 16384)
    pT = at("pT", [128, 2, 2, 4, 128], BF16, STG + 32768)
    outt = at("outt", [128, 2, DM], F32, STG + 36864)
    xrs = at("xrs", [128, 2, DM], F32, STG + 45056)
    sg_t = at("sg_t", [128, DM], F32, STG + 53248)

    psA = nc.alloc_psum_tensor("psA", [128, 2048], F32)
    psBC = nc.alloc_psum_tensor("psBC", [128, 1024], F32)
    psD = nc.alloc_psum_tensor("psD", [128, 1024], F32)
    trA = psA.bitcast(BF16)
    trBC = psBC.bitcast(BF16)

    E = Emit(nc)

    for w_h, w_b in ((wk_h, wk_b), (wv_h, wv_b), (wq_h, wq_b)):
        E.qdma("pl", "wqkv", lambda g, w_h=w_h, w_b=w_b: g.dma_start(
            out=w_b[:], in_=w_h.rearrange("(c p) e -> p c e", p=128)))
    E.qdma("sy", "const", lambda s: s.dma_start(out=ident[:], in_=ident_h[:]))
    E.qdma("sy", "const", lambda s: s.dma_start(out=maskb[:], in_=mask_h[:]))
    sp_const = E.mark("const")

    if nphase >= 1:
        wts_marks = []
        wts_dma = []
        wstf = wst.ap().rearrange("p a b -> p (a b)")
        tr_done = []
        for wi, (w_h, wT_s) in enumerate(
                ((wg_h, wgT_s), (wu_h, wuT_s), (wd_h, wdT_s))):
            is_d = wi == 2
            nrow = DC if is_d else HC
            ncol = HC if is_d else DC
            wfree = HID if is_d else DM
            hr = nrow // 2
            for hf in range(2):
                hpar = (wi * 2 + hf) % 2
                r_lo = hf * hr
                if len(tr_done) >= 2:
                    E.qwait("pl", "t", tr_done[-2])
                E.qdma("pl", "wst%d" % hpar, lambda g, w_h=w_h, r_lo=r_lo,
                       hr=hr, wfree=wfree, hpar=hpar: g.dma_start(
                           out=wstf[:, hpar * 12288:
                                    hpar * 12288 + hr * wfree].rearrange(
                               "p (a b) -> p a b", b=wfree),
                           in_=w_h[r_lo * 128:(r_lo + hr) * 128, :].rearrange(
                               "(c p) e -> p c e", p=128)))
                gp_m = E.mark("wst%d" % hpar)

                for cb in range(ncol):
                    nunit = (hr + 7) // 8
                    for u in range(nunit):
                        r0, r1 = u * 8, min(hr, u * 8 + 8)
                        par = (cb * nunit + u) % 4
                        E.qwait("t", "const", sp_const)
                        E.qwait("t", "wst%d" % hpar, gp_m)
                        if len(wts_marks) >= 4:
                            E.qwait("t", "a", wts_marks[-4])
                        mk = E.qgroup("t", [
                            (lambda t, rb=rb, r0=r0, cb=cb, par=par,
                             wfree=wfree, hpar=hpar: t.transpose(
                                 trA[:, par * 1024 + (rb - r0) * 128:
                                     par * 1024 + (rb - r0 + 1) * 128],
                                 wstf[:, hpar * 12288 + rb * wfree + cb * 128:
                                      hpar * 12288 + rb * wfree +
                                      (cb + 1) * 128],
                                 ident[:]))
                            for rb in range(r0, r1)], "t")

                        E.qwait("a", "t")
                        if len(wts_dma) >= 4:
                            E.qwait("a", *wts_dma[-4])
                        wts_marks.append(E.qsop("a", lambda a, r0=r0, r1=r1,
                                                par=par, wpar=cb % 4:
                                                a.activation(
                            out=wts[:, wpar, r0 * 128:r1 * 128],
                            in_=trA[:, par * 1024:
                                    par * 1024 + (r1 - r0) * 128],
                            func=AF.Copy)))

                    E.qwait("a", "a")
                    wts_dma.append(("wTs%d" % (cb % 4), E.qdma(
                        "a", "wTs%d" % (cb % 4), lambda a, cb=cb, hr=hr,
                        r_lo=r_lo, wT_s=wT_s, wpar=cb % 4: a.dma_start(
                            out=wT_s[cb * 128:(cb + 1) * 128,
                                     r_lo * 128:(r_lo + hr) * 128],
                            in_=wts[:, wpar, 0:hr * 128]))))
                tr_done.append(mk)

    if nphase >= 2:
        nq_tiles = 16 if nphase >= 3 else 0

        aggr_m = []

        nmark = []
        entries = [(xk_h, b, True) for b in range(16)]
        if nq_tiles:
            entries += [(xq_h, b, False) for b in range(4)]
        NB = len(entries)

        tr_evac = []
        tr_tm = []
        k_evac = []
        v_evac = []
        kv_tmark = []
        rstd_ms = {}
        ld_ms = {}

        def emit_load(gb):
            src_h, lb, kv = entries[gb]
            bpar = gb % 2
            if gb >= 2:
                E.qwait("sy", "v", nmark[-1])
            E.qdma("sy", "xkb4_%d" % bpar,
                   lambda s, lb=lb, bpar=bpar, src_h=src_h: s.dma_start(
                       out=xkb4[:, bpar, :, :],
                       in_=src_h[lb * 512:(lb + 1) * 512, :].rearrange(
                           "(ti p) d -> p ti d", p=128)))
            ld_ms[gb] = E.mark("xkb4_%d" % bpar)

        def emit_stats(gb):
            bpar = gb % 2
            g4 = 4 * gb
            E.qwait("v", "xkb4_%d" % bpar, ld_ms[gb])
            for ti in range(4):
                b2 = ti % 2
                E.qsop("v", lambda v, bpar=bpar, ti=ti, b2=b2: v.bn_stats(
                    bn6[:, b2, 0:6], xkb4[:, bpar, ti, 0:512]),
                    dep=(aggr_m[-2] if len(aggr_m) >= 2 else 0))
                E.qsop("v", lambda v, bpar=bpar, ti=ti, b2=b2: v.bn_stats(
                    bn6[:, b2, 6:12], xkb4[:, bpar, ti, 512:1024]),
                    dep=-10**9)
                aggr_m.append(E.qsop("v", lambda v, g4=g4, ti=ti, b2=b2:
                                     v.bn_aggr(
                    mvall[:, g4 + ti, :], bn6[:, b2, :])))
            E.qsop("v", lambda v, g4=g4: v.tensor_scalar_add(
                veps[:, g4:g4 + 4], mvall[:, g4:g4 + 4, 1:2], EPS))
            E.qsop("v", lambda v, g4=g4: v.tensor_scalar(
                out=rstdall[:, g4:g4 + 4], in0=veps[:, g4:g4 + 4],
                scalar1=0.5, scalar2=0.5, op0=ALU.mult, op1=ALU.add))
            E.qsop("v", lambda v, g4=g4: v.reciprocal(
                rstdall[:, g4:g4 + 4], rstdall[:, g4:g4 + 4]))
            for _ in range(5):
                E.qsop("v", lambda v, g4=g4: v.tensor_mul(
                    fsb[:, 0, 0:4], rstdall[:, g4:g4 + 4],
                    rstdall[:, g4:g4 + 4]))
                E.qsop("v", lambda v, g4=g4: v.tensor_mul(
                    fsb[:, 0, 0:4], fsb[:, 0, 0:4], veps[:, g4:g4 + 4]))
                E.qsop("v", lambda v: v.tensor_scalar(
                    out=fsb[:, 0, 0:4], in0=fsb[:, 0, 0:4],
                    scalar1=-0.5, scalar2=1.5, op0=ALU.mult, op1=ALU.add))
                E.qsop("v", lambda v, g4=g4: v.tensor_mul(
                    rstdall[:, g4:g4 + 4], rstdall[:, g4:g4 + 4],
                    fsb[:, 0, 0:4]))
            rstd_ms[gb] = E.mark("v")

        xnsm = {}

        def emit_ladder(gb):
            src_h, lb, kv = entries[gb]
            bpar = gb % 2
            pb = gb % 2
            g4 = 4 * gb
            for ti in range(4):
                tile = 4 * lb + ti
                par = ti % 2
                if len(tr_tm) >= 2:
                    E.qwait("v", "t", tr_tm[-2])
                v_m = E.qsop("v", dep=rstd_ms[gb], fn=lambda v,
                             bpar=bpar, ti=ti, par=par,
                             g4=g4: v.tensor_scalar(
                    out=xnr[:, par, :], in0=xkb4[:, bpar, ti, :],
                    scalar1=mvall[:, g4 + ti, 0:1],
                    scalar2=rstdall[:, g4 + ti:g4 + ti + 1],
                    op0=ALU.subtract, op1=ALU.mult))

                if ti == 3:
                    nmark.append(v_m)
                E.qwait("t", "v", v_m)
                if len(tr_evac) >= 2:
                    E.qwait("t", "a", tr_evac[-2])
                tr_tm.append(E.qgroup("t", [
                    (lambda t, dcc=dcc, par=par: t.transpose(
                        trA[:, par * 1024 + dcc * 128:
                            par * 1024 + (dcc + 1) * 128],
                        xnr[:, par, dcc * 128:(dcc + 1) * 128],
                        ident[:]))
                    for dcc in range(DC)], "t"))

                E.qwait("a", "t")
                if kv:
                    fn = (lambda a, pb=pb, ti=ti, par=par: a.activation(
                        out=xns[:, pb, :, ti * 128:(ti + 1) * 128],
                        in_=trA[:, par * 1024:(par + 1) * 1024]
                        .rearrange("p (c q) -> p c q", q=128),
                        func=AF.Copy))
                else:
                    fn = (lambda a, tile=tile, par=par: a.activation(
                        out=xnq[:, :, tile * 128:(tile + 1) * 128],
                        in_=trA[:, par * 1024:(par + 1) * 1024]
                        .rearrange("p (c q) -> p c q", q=128),
                        func=AF.Copy))
                tr_evac.append(E.qsop("a", fn))
            xnsm[gb] = E.mark("a")

        def emit_K(gb):
            src_h, lb, kv = entries[gb]
            pb = gb % 2
            a_xns = xnsm[gb]
            for e in range(DC):
                ph = e % 2
                E.qwait("t", "a", a_xns)
                if len(k_evac) >= 2:
                    E.qwait("t", "a", k_evac[-2])
                kv_tmark.append(E.qgroup("t", [
                    (lambda t, dcc=dcc, pb=pb, e=e, ph=ph: t.matmul(
                        psBC[:, ph * 512:(ph + 1) * 512],
                        wk_b[:, dcc, e * 128:(e + 1) * 128],
                        xns[:, pb, dcc, :],
                        start=(dcc == 0), stop=(dcc == DC - 1)))
                    for dcc in range(DC)], "t"))

                E.qwait("a", "t", kv_tmark[-1])
                E.qwait("a", "kst")
                k_evac.append(E.qsop("a", lambda a, e=e, ph=ph:
                                     a.activation(
                    out=kst[:, e, :],
                    in_=psBC[:, ph * 512:(ph + 1) * 512],
                    func=AF.Copy)))
            E.qwait("pl", "a")
            E.qdma("pl", "kst", lambda g, lb=lb: g.dma_start(
                out=kT_s[lb],
                in_=kst.ap().rearrange("p c q -> p (c q)")))

        def emit_V(gb):
            src_h, lb, kv = entries[gb]
            pb = gb % 2
            for sti in range(4):
                vtgt = psD[:, 0:1024] if sti % 2 == 0                     else psA[:, 1024:2048]
                if len(v_evac) >= 2:
                    E.qwait("t", "a", v_evac[-2])
                fns = []
                for dcc in range(DC):
                    for eh in range(2):
                        fns.append(lambda t, dcc=dcc, eh=eh, pb=pb,
                                   sti=sti, vtgt=vtgt: t.matmul(
                            vtgt[:, eh * 512:(eh + 1) * 512],
                            xns[:, pb, dcc,
                                sti * 128:(sti + 1) * 128],
                            wv_b[:, dcc, eh * 512:(eh + 1) * 512],
                            start=(dcc == 0), stop=(dcc == DC - 1)))
                kv_tmark.append(E.qgroup("t", fns, "t"))

                E.qwait("a", "t", kv_tmark[-1])
                E.qwait("a", "vst")
                v_evac.append(E.qsop("a", lambda a, sti=sti,
                                     vtgt=vtgt: a.activation(
                    out=vst[:, sti, :], in_=vtgt, func=AF.Copy)))

            E.qwait("pl", "a")
            E.qdma("pl", "vst", lambda g, lb=lb: g.dma_start(
                out=vv_s[lb],
                in_=vst.ap().rearrange("p t e -> p (t e)")))

        def emit_Q(sb):
            for e in range(DC):
                ph = e % 2
                E.qwait("t", "a", xnsm[16 + sb])
                if len(k_evac) >= 2:
                    E.qwait("t", "a", k_evac[-2])
                kv_tmark.append(E.qgroup("t", [
                    (lambda t, dcc=dcc, sb=sb, e=e, ph=ph: t.matmul(
                        psBC[:, ph * 512:(ph + 1) * 512],
                        wq_b[:, dcc, e * 128:(e + 1) * 128],
                        xnq[:, dcc, sb * 512:(sb + 1) * 512],
                        start=(dcc == 0), stop=(dcc == DC - 1)))
                    for dcc in range(DC)], "t"))
                E.qwait("a", "t", kv_tmark[-1])
                k_evac.append(E.qsop("a", lambda a, sb=sb, e=e, ph=ph:
                                     a.activation(
                    out=qT[:, e, sb * 512:(sb + 1) * 512],
                    in_=psBC[:, ph * 512:(ph + 1) * 512],
                    func=AF.Copy)))

        E.qwait("t", "a")
        E.qwait("t", "wqkv")
        emit_load(0)
        emit_stats(0)
        emit_load(1)
        emit_stats(1)
        emit_ladder(0)
        emit_load(2)
        for gb in range(NB):
            kv = entries[gb][2]
            boundary = kv and gb + 1 < NB and not entries[gb + 1][2]
            if kv:
                emit_K(gb)
            elif nphase >= 4:
                emit_Q(gb - 16)
            if not boundary:
                if gb + 1 < NB:
                    emit_ladder(gb + 1)
                if gb + 3 < NB:
                    emit_load(gb + 3)
                if gb + 2 < NB:
                    emit_stats(gb + 2)
            if kv:
                emit_V(gb)
            if boundary:
                emit_ladder(gb + 1)
                if gb + 3 < NB:
                    emit_load(gb + 3)
                if gb + 2 < NB:
                    emit_stats(gb + 2)

    E.barrier()

    E.barrier()

    if nphase >= 5:
        E.qdma("sy", "wdT", lambda s: s.dma_start(
            out=wdT_b[:], in_=wdT_s.rearrange("(c p) e -> p c e", p=128)))
        sp_wd = E.mark("wdT")

        gu_tmark = []
        hm_vmark = []
        si_amark = []
        g_amark = []
        dn_tmark = []
        for q in range(2):
            qs = q * 1024
            for hb in range(HC):
                par = hb % 2
                if len(gu_tmark) >= 2:
                    E.qwait("sy", "t", gu_tmark[-2][1])
                E.qdma("sy", "wgu%d" % par, lambda s, hb=hb, par=par:
                       s.dma_start(
                           out=wgu[:, par, 0, :, :],
                           in_=wgT_s.rearrange("(c p) h -> p c h", p=128)
                           [:, :, hb * 128:(hb + 1) * 128]))
                E.qdma("sy", "wgu%d" % par, lambda s, hb=hb, par=par:
                       s.dma_start(
                           out=wgu[:, par, 1, :, :],
                           in_=wuT_s.rearrange("(c p) h -> p c h", p=128)
                           [:, :, hb * 128:(hb + 1) * 128]))
                sp_m = E.mark("wgu%d" % par)

                gb = (hb % 2) * 1024
                upD = hb % 2 == 1
                E.qwait("t", "wgu%d" % par, sp_m)
                if len(si_amark) >= 2:
                    E.qwait("t", "a", si_amark[-2])
                if len(hm_vmark) >= 2:
                    E.qwait("t", "v", hm_vmark[-2])
                if upD and g_amark:
                    E.qwait("t", "a", g_amark[-1])
                fns = []
                for dcc in range(DC):
                    for sh in range(2):
                        fns.append(lambda t, dcc=dcc, sh=sh, qs=qs, par=par,
                                   gb=gb: t.matmul(
                            psA[:, gb + sh * 512:gb + (sh + 1) * 512],
                            wgu[:, par, 0, dcc, :],
                            xnq[:, dcc, qs + sh * 512:qs + (sh + 1) * 512],
                            start=(dcc == 0), stop=(dcc == DC - 1)))
                m_g = E.qgroup("t", fns, "t")
                fns = []
                for dcc in range(DC):
                    for sh in range(2):
                        fns.append(lambda t, dcc=dcc, sh=sh, qs=qs, par=par,
                                   upD=upD: t.matmul(
                            (psD if upD else psBC)[:, sh * 512:
                                                   (sh + 1) * 512],
                            wgu[:, par, 1, dcc, :],
                            xnq[:, dcc, qs + sh * 512:qs + (sh + 1) * 512],
                            start=(dcc == 0), stop=(dcc == DC - 1)))
                m_u = E.qgroup("t", fns, "t")
                gu_tmark.append((m_g, m_u))

                E.qwait("a", "t", m_g)
                if hm_vmark:
                    E.qwait("a", "v", hm_vmark[-1])
                si_amark.append(E.qsop("a", lambda a, gb=gb: a.activation(
                    out=silu_t[:], in_=psA[:, gb:gb + 1024],
                    func=AF.Silu, scale=SCALE)))

                E.qwait("v", "t", m_u)
                E.qwait("v", "a", si_amark[-1])
                hm_vmark.append(E.qsop("v", lambda v, hb=hb, upD=upD:
                                       v.tensor_tensor(
                    hact[:, hb, :], silu_t[:], (psD if upD else psBC)[:],
                    ALU.mult)))
            v_hact = E.mark("v")

            for sti in range(8):
                E.qwait("t", "v", v_hact)
                E.qwait("t", "wdT", sp_wd)
                if len(g_amark) >= 1:
                    E.qwait("t", "a", g_amark[-1])
                fns = []
                for hb in range(HC):
                    for eh in range(2):
                        fns.append(lambda t, hb=hb, sti=sti, eh=eh: t.matmul(
                            psD[:, eh * 512:(eh + 1) * 512],
                            hact[:, hb, sti * 128:(sti + 1) * 128],
                            wdT_b[:, hb, eh * 512:(eh + 1) * 512],
                            start=(hb == 0), stop=(hb == HC - 1)))
                dn_tmark.append(E.qgroup("t", fns, "t"))

                E.qwait("a", "t", dn_tmark[-1])
                g_amark.append(E.qsop("a", lambda a, q=q, sti=sti:
                                      a.activation(
                    out=g_b[:, q * 8 + sti, :], in_=psD[:],
                    func=AF.Silu, scale=SCALE * SCALE)))

    E.barrier()

    if nphase >= 6:
        cmb_vm = []
        kbuf_last = {}
        vread_m = {}
        pt_last = {}
        score_free = {}
        trb_free = {}
        obank_free = []
        lin_prev = [0]

        for pair in range(NS // 2):
            kA, kB = 2 * pair, 2 * pair + 1
            nsupA = kA + 1
            nsupB = kB + 1
            pbufs = (pbufA, pbufB)
            tr_m = {}
            pte_m = {}
            exp_m = {}
            av_m = {}
            vdma_m = {}

            for sl, kk in ((0, kA), (1, kB)):
                if cmb_vm:
                    E.qwait("pl", "v", cmb_vm[-1])
                E.qdma("pl", "xrs%d" % (kk % 2), lambda g, kk=kk: g.dma_start(
                    out=xrs[:, kk % 2, :],
                    in_=xq_h[kk * 128:(kk + 1) * 128, :]))
            xr_m = (E.mark("xrs0"), E.mark("xrs1"))

            def slots_of(s):
                return [0, 1] if s < nsupA else [1]

            def emit_tr(s):
                tb = 2048 + (s % 2) * 1024
                slots = slots_of(s)
                for sl in slots:
                    E.qwait("t", "a", exp_m[(sl, s)])
                fw = trb_free.get(s % 2)
                if fw is not None:
                    E.qwait("t", "a", fw)
                tr_m[s] = E.qgroup("t", [
                    (lambda t, sl=sl, kt=kt, s=s, tb=tb: t.transpose(
                        trA[:, tb + sl * 512 + kt * 128:
                            tb + sl * 512 + (kt + 1) * 128],
                        pbufs[sl][:, s * 512 + kt * 128:
                                  s * 512 + (kt + 1) * 128],
                        ident[:]))
                    for sl in slots for kt in range(4)], "t")

            def emit_pte(s):
                ph = s % 2
                tb = 2048 + (s % 2) * 1024
                s0 = 0 if s < nsupA else 1
                E.qwait("a", "t", tr_m[s])
                pw = pt_last.get(ph)
                if pw is not None:
                    E.qwait("a", "t", pw)
                pte_m[s] = E.qsop("a", lambda a, ph=ph, tb=tb, s0=s0:
                                  a.activation(
                    out=pT[:, ph, s0:2, :, :],
                    in_=trA[:, tb + s0 * 512:tb + 1024]
                    .rearrange("p (s t q) -> p s t q", t=4, q=128),
                    func=AF.Copy))
                trb_free[s % 2] = pte_m[s]

            def emit_av(s):
                ph = s % 2
                E.qwait("t", "a", pte_m[s])
                E.qwait("t", "vbuf%d" % ph, vdma_m[s])
                if s == 0:
                    for bw in obank_free:
                        E.qwait("t", "v", bw)
                    del obank_free[:]
                fns = []
                for sl in slots_of(s):
                    tgt = psBC if sl == 0 else psD
                    nv = nsupA if sl == 0 else nsupB
                    for kt in range(4):
                        for eh in range(2):
                            fns.append(lambda t, sl=sl, kt=kt, eh=eh, s=s,
                                       ph=ph, tgt=tgt, nv=nv: t.matmul(
                                tgt[:, eh * 512:(eh + 1) * 512],
                                pT[:, ph, sl, kt, :],
                                vbuf[:, ph, kt, eh * 512:(eh + 1) * 512],
                                start=(s == 0 and kt == 0),
                                stop=(s == nv - 1 and kt == 3)))
                av_m[s] = E.qgroup("t", fns, "t")
                vread_m[ph] = av_m[s]
                pt_last[ph] = av_m[s]

            for s in range(nsupB):
                ph4 = s % 4
                kb_t = kbuf if ph4 < 2 else kbufX
                kb_ap = kb_t[:, ph4 % 2, :, :]
                kb_w = kbuf_last.get(ph4)
                if kb_w is not None:
                    E.qwait("sy", "t", kb_w)
                E.qdma("sy", "kbuf%d" % ph4, lambda s_, sup=s, kb_ap=kb_ap:
                       s_.dma_start(
                           out=kb_ap,
                           in_=kT_s[sup].rearrange("p (c q) -> p c q",
                                                   q=512)))
                kdma = E.mark("kbuf%d" % ph4)

                vph = s % 2
                vw = vread_m.get(vph)
                if vw is not None:
                    E.qwait("pl", "t", vw)
                E.qdma("pl", "vbuf%d" % vph, lambda g, sup=s, vph=vph:
                       g.dma_start(
                           out=vbuf[:, vph, :, :],
                           in_=vv_s[sup].rearrange("p (t e) -> p t e",
                                                   e=DM)))
                vdma_m[s] = E.mark("vbuf%d" % vph)

                if s >= 1:
                    emit_tr(s - 1)

                slots = slots_of(s)
                E.qwait("t", "kbuf%d" % ph4, kdma)
                for sl in slots:
                    fw = score_free.get(sl)
                    if fw is not None:
                        E.qwait("t", "a", fw)
                fns = []
                for sl in slots:
                    kk = kA if sl == 0 else kB
                    for e in range(DC):
                        fns.append(lambda t, e=e, kk=kk, sl=sl, kb_t=kb_t,
                                   php=ph4 % 2: t.matmul(
                            psA[:, sl * 512:(sl + 1) * 512],
                            qT[:, e, kk * 128:(kk + 1) * 128],
                            kb_t[:, php, e, :],
                            start=(e == 0), stop=(e == DC - 1)))
                qk_m = E.qgroup("t", fns, "t")
                kbuf_last[ph4] = qk_m

                msk_m = {}
                for sl in slots:
                    if s == (nsupA if sl == 0 else nsupB) - 1:
                        E.qwait("v", "t", qk_m)
                        msk_m[sl] = E.qsop("v", lambda v, sl=sl:
                                           v.tensor_tensor(
                            psA[:, sl * 512:(sl + 1) * 512],
                            psA[:, sl * 512:(sl + 1) * 512],
                            maskb[:], ALU.add))

                if s >= 1:
                    emit_pte(s - 1)

                E.qwait("a", "t", qk_m)
                if s == 0:
                    E.qwait("a", "v", lin_prev[0])
                for sl in slots:
                    if sl in msk_m:
                        E.qwait("a", "v", msk_m[sl])
                    exp_m[(sl, s)] = E.qsop("a", lambda a, sl=sl, sup=s:
                                            a.activation(
                        out=pbufs[sl][:, sup * 512:(sup + 1) * 512],
                        in_=psA[:, sl * 512:(sl + 1) * 512],
                        func=AF.Exp, scale=1.0 / 32.0,
                        accum_out=lsb[:, sl, sup:sup + 1]))
                    score_free[sl] = exp_m[(sl, s)]

                if s >= 1:
                    emit_av(s - 1)

            emit_tr(nsupB - 1)
            emit_pte(nsupB - 1)
            emit_av(nsupB - 1)

            E.qwait("v", "a")
            for sl, nsup in ((0, nsupA), (1, nsupB)):
                E.qsop("v", lambda v, sl=sl, nsup=nsup: v.reduce_sum(
                    lsum[:, sl:sl + 1], lsb[:, sl, 0:nsup], axis=AX.X))
            lin_prev[0] = E.qsop("v", lambda v: v.reciprocal(linv[:],
                                                            lsum[:]))

            for sl, kk in ((0, kA), (1, kB)):
                av_last = av_m[(nsupA if sl == 0 else nsupB) - 1]
                E.qwait("v", "t", av_last)
                E.qwait("v", "xrs%d" % (kk % 2), xr_m[kk % 2])
                E.qwait("v", "out%d" % (kk % 2))
                src_ps = psBC if sl == 0 else psD
                E.qsop("v", lambda v, kk=kk, sl=sl, src_ps=src_ps:
                       v.scalar_tensor_tensor(
                    out=sg_t[:], in0=src_ps[:], scalar=linv[:, sl:sl + 1],
                    in1=g_b[:, kk, :], op0=ALU.mult, op1=ALU.mult))
                m = E.qsop("v", lambda v, kk=kk: v.tensor_add(
                    outt[:, kk % 2, :], sg_t[:], xrs[:, kk % 2, :]))
                cmb_vm.append(m)
                obank_free.append(m)

                E.qwait("pl", "v", m)
                E.qdma("pl", "out%d" % (kk % 2), lambda g, kk=kk:
                       g.dma_start(
                    out=out_h[kk * 128:(kk + 1) * 128, :],
                    in_=outt[:, kk % 2, :]))

        E.qwait("pl", "out0")
        E.qwait("pl", "out1")

    with nc.Block() as blk:
        E.flush(blk)
    E.es.close()
    return nc


_NC_CACHE = {}


def _mk_inputs(X, wq, wk, wv, wg, wu, wd):
    ident = np.eye(128, dtype=np.float32).astype(BF)
    in_maps = []
    for core in range(8):
        b, j = core // 4, core % 4
        rows = np.arange(NS) * 4 + j
        xq = np.ascontiguousarray(
            X[b].reshape(64, 128, DM)[rows].reshape(QR, DM))
        c = np.arange(512)[None, :]
        r = np.arange(128)[:, None]
        mask = np.where(c <= 128 * j + r, 0.0, NEG).astype(np.float32)
        in_maps.append({
            "xk": X[b], "xq": xq,
            "wq": wq, "wk": wk, "wv": wv,
            "wg": wg, "wu": wu, "wd": wd,
            "mask": mask, "ident": ident,
        })
    return in_maps


def kernel(L0_attn_X, L0_attn_Wq, L0_attn_Wk, L0_attn_Wv,
           L0_ffn_w_gate, L0_ffn_w_up, L0_ffn_w_down):
    X = np.ascontiguousarray(np.asarray(L0_attn_X, dtype=np.float32))
    wq = np.ascontiguousarray(np.asarray(L0_attn_Wq, dtype=np.float32))
    wk = np.ascontiguousarray(np.asarray(L0_attn_Wk, dtype=np.float32))
    wv = np.ascontiguousarray(np.asarray(L0_attn_Wv, dtype=np.float32))
    wg = np.ascontiguousarray(np.asarray(L0_ffn_w_gate, dtype=np.int32))
    wu = np.ascontiguousarray(np.asarray(L0_ffn_w_up, dtype=np.int32))
    wd = np.ascontiguousarray(np.asarray(L0_ffn_w_down, dtype=np.int32))

    if "nc" not in _NC_CACHE:
        _NC_CACHE["nc"] = build()
    nc = _NC_CACHE["nc"]
    in_maps = _mk_inputs(X, wq, wk, wv, wg, wu, wd)
    res = run_bass_kernel_spmd(nc, in_maps, core_ids=list(range(8)))

    out = np.empty((2, S, DM), dtype=np.float32)
    for core in range(8):
        b, j = core // 4, core % 4
        rows = np.arange(NS) * 4 + j
        out[b].reshape(64, 128, DM)[rows] = \
            res.results[core]["out"].reshape(NS, 128, DM)
    return out


# version 26
# speedup vs baseline: 1.2355x; 1.0112x over previous
import contextlib
import numpy as np
import ml_dtypes
import concourse.bass as bass
import concourse.mybir as mybir
from concourse.bass_utils import run_bass_kernel_spmd

F32 = mybir.dt.float32
BF16 = mybir.dt.bfloat16
I32 = mybir.dt.int32
AF = mybir.ActivationFunctionType
ALU = mybir.AluOpType
AX = mybir.AxisListType
BF = ml_dtypes.bfloat16

S, DM, HID = 8192, 1024, 3072
NS = 16
QR = NS * 128
DC = DM // 128
HC = HID // 128
SCALE = 1.0 / 127.0
EPS = 1e-5
NEG = -30000.0
SELF_SYNC = set()
DBG_STATS = False
FAST_OK = set()


class Emit:

    def __init__(self, nc):
        self.nc = nc
        self.es = contextlib.ExitStack()
        self.sem = {}
        self.cnt = {}
        self.qs = {k: [] for k in ("sy", "pl", "v", "a", "t")}
        for name in ("v", "a", "t"):
            self._mk(name)

    def _mk(self, name):
        if name not in self.sem:
            self.sem[name] = self.es.enter_context(
                self.nc.semaphore("sem_" + name))
            self.cnt[name] = 0
        return self.sem[name]

    def qwait(self, ek, semkey, val=None):
        self._mk(semkey)
        v = self.cnt[semkey] if val is None else val
        self.qs[ek].append(("w", self.sem[semkey], v))

    def qop(self, ek, fn, inckey=None):
        mark = None
        if inckey is not None:
            self._mk(inckey)
            self.cnt[inckey] += 1
            mark = self.cnt[inckey]
            self.qs[ek].append(("o", fn, self.sem[inckey], 1))
        else:
            self.qs[ek].append(("o", fn, None, 0))
        return mark

    def qsop(self, ek, fn=None, fast=False, dep=None):
        key = {"v": "v", "a": "a", "t": "t"}[ek]
        if key == "v" and (fast is False or fast not in FAST_OK):
            if not (isinstance(dep, int) and dep < 0):
                self.qwait(ek, key, dep)
        return self.qop(ek, fn, key)

    def qdma(self, ek, semkey, fn):
        self._mk(semkey)
        self.cnt[semkey] += 16
        mark = self.cnt[semkey]
        self.qs[ek].append(("o", fn, self.sem[semkey], 16))
        return mark

    def qgroup(self, ek, fns, inckey):
        self._mk(inckey)
        self.cnt[inckey] += 1
        mark = self.cnt[inckey]
        self.qs[ek].append(("g", fns, self.sem[inckey], 1))
        return mark

    def mark(self, key):
        self._mk(key)
        return self.cnt[key]

    def barrier(self):
        marks = dict(self.cnt)
        for ek in self.qs:
            for kk, vv in marks.items():
                self.qs[ek].append(("w", self.sem[kk], vv))

    def flush(self, blk):
        def emit(eng, items):
            for it in items:
                if it[0] == "w":
                    eng.wait_ge(it[1], it[2])
                elif it[0] == "o":
                    ins = it[1](eng)
                    if it[2] is not None:
                        ins.then_inc(it[2], it[3])
                else:
                    for fn in it[1]:
                        ins = fn(eng)
                    ins.then_inc(it[2], it[3])

        @blk.sync
        def _(s):
            emit(s, self.qs["sy"])

        @blk.gpsimd
        def _(g):
            emit(g, self.qs["pl"])

        @blk.vector
        def _(v):
            emit(v, self.qs["v"])

        @blk.scalar
        def _(a):
            emit(a, self.qs["a"])

        @blk.tensor
        def _(t):
            emit(t, self.qs["t"])


def build(nphase=9):
    nc = bass.Bass("TRN2", target_bir_lowering=False, debug=False)

    xk_h = nc.dram_tensor("xk", [S, DM], F32, kind="ExternalInput").ap()
    xq_h = nc.dram_tensor("xq", [QR, DM], F32, kind="ExternalInput").ap()
    wq_h = nc.dram_tensor("wq", [DM, DM], F32, kind="ExternalInput").ap()
    wk_h = nc.dram_tensor("wk", [DM, DM], F32, kind="ExternalInput").ap()
    wv_h = nc.dram_tensor("wv", [DM, DM], F32, kind="ExternalInput").ap()
    wg_h = nc.dram_tensor("wg", [HID, DM], I32, kind="ExternalInput").ap()
    wu_h = nc.dram_tensor("wu", [HID, DM], I32, kind="ExternalInput").ap()
    wd_h = nc.dram_tensor("wd", [DM, HID], I32, kind="ExternalInput").ap()
    mask_h = nc.dram_tensor("mask", [128, 512], F32, kind="ExternalInput").ap()
    ident_h = nc.dram_tensor("ident", [128, 128], BF16, kind="ExternalInput").ap()
    out_h = nc.dram_tensor("out", [QR, DM], F32, kind="ExternalOutput").ap()

    kT_s = nc.dram_tensor("kT_s", [S // 512, 128, DC * 512], BF16,
                          kind="Internal").ap()
    vv_s = nc.dram_tensor("vv_s", [S // 512, 128, 4 * DM], BF16,
                          kind="Internal").ap()
    wgT_s = nc.dram_tensor("wgT_s", [DM, HID], BF16, kind="Internal").ap()
    wuT_s = nc.dram_tensor("wuT_s", [DM, HID], BF16, kind="Internal").ap()
    wdT_s = nc.dram_tensor("wdT_s", [HID, DM], BF16, kind="Internal").ap()

    BASE = 16896

    def at(name, shape, dtype, off):
        return nc.alloc_sbuf_tensor_at(name, shape, dtype, offset=BASE + off)

    wq_b = at("wq_b", [128, DC, DM], BF16, 0)
    wk_b = at("wk_b", [128, DC, DM], BF16, 16384)
    wv_b = at("wv_b", [128, DC, DM], BF16, 32768)
    xnq = at("xnq", [128, DC, QR], BF16, 49152)
    qT = at("qT", [128, DC, QR], BF16, 81920)
    g_b = at("g_b", [128, NS, DM], BF16, 114688)
    ident = at("ident_b", [128, 128], BF16, 147456)
    maskb = at("maskb", [128, 512], F32, 147712)
    bn6 = at("bn6", [128, 2, 12], F32, 149760)
    msb = at("msb", [128, 2, 16], F32, 149824)
    lsb = at("lsb", [128, 2, 16], F32, 149952)
    fsb = at("fsb", [128, 2, 16], F32, 150080)
    mglob = at("mglob", [128, 2], F32, 150208)
    lsum = at("lsum", [128, 2], F32, 150240)
    linv = at("linv", [128, 2], F32, 150272)
    veps = at("veps", [128, 80], F32, 150304)
    STG = 152064

    wst = at("wst", [128, HC, DM], BF16, 81920)
    wts = at("wts", [128, 4, 1536], BF16, 131072)
    xkb4 = at("xkb4", [128, 2, 4, DM], F32, STG)
    xnr = at("xnr", [128, 2, DM], BF16, STG + 32768)
    kst = at("kst", [128, DC, 512], BF16, STG + 36864)
    vst = at("vst", [128, 4, DM], BF16, STG + 45056)
    xns = at("xns", [128, 2, DC, 512], BF16, 49152)
    mvall = at("mvall", [128, 80, 2], F32, 150656)
    rstdall = at("rstdall", [128, 80], F32, 151296)
    wdT_b = at("wdT_b", [128, HC, DM], BF16, 0)
    hact = at("hact", [128, HC, DM], BF16, STG)
    wgu = at("wgu", [128, 2, 2, DC, 128], BF16, STG + 49152)
    silu_t = at("silu_t", [128, DM], BF16, STG + 57344)
    pbufA = at("pbufA", [128, S], BF16, 0)
    pbufB = at("pbufB", [128, S], BF16, 49152)
    kbufX = at("kbufX", [128, 2, DC, 512], BF16, 65536)
    kbuf = at("kbuf", [128, 2, DC, 512], BF16, STG)
    vbuf = at("vbuf", [128, 2, 4, DM], BF16, STG + 16384)
    pT = at("pT", [128, 2, 2, 4, 128], BF16, STG + 32768)
    outt = at("outt", [128, 2, DM], F32, STG + 36864)
    xrs = at("xrs", [128, 2, DM], F32, STG + 45056)
    sg_t = at("sg_t", [128, DM], F32, STG + 53248)

    psA = nc.alloc_psum_tensor("psA", [128, 2048], F32)
    psBC = nc.alloc_psum_tensor("psBC", [128, 1024], F32)
    psD = nc.alloc_psum_tensor("psD", [128, 1024], F32)
    trA = psA.bitcast(BF16)
    trBC = psBC.bitcast(BF16)

    E = Emit(nc)

    for w_h, w_b in ((wk_h, wk_b), (wv_h, wv_b), (wq_h, wq_b)):
        E.qdma("pl", "wqkv", lambda g, w_h=w_h, w_b=w_b: g.dma_start(
            out=w_b[:], in_=w_h.rearrange("(c p) e -> p c e", p=128)))
    E.qdma("sy", "const", lambda s: s.dma_start(out=ident[:], in_=ident_h[:]))
    E.qdma("sy", "const", lambda s: s.dma_start(out=maskb[:], in_=mask_h[:]))
    sp_const = E.mark("const")

    rot = []
    wtasks = []
    wst_m = {}
    half_trm = {}
    wts_dma = []
    wts_final = {}
    wstf = wst.ap().rearrange("p a b -> p (a b)")
    halves = [(wi, hf) for wi in range(3) for hf in range(2)]

    def _half_params(h):
        wi, hf = halves[h]
        is_d = wi == 2
        nrow = DC if is_d else HC
        ncol = HC if is_d else DC
        wfree = HID if is_d else DM
        hr = nrow // 2
        return wi, hf, ncol, wfree, hr, hf * hr

    def issue_half_load(h):
        wi, hf, ncol, wfree, hr, r_lo = _half_params(h)
        w_h = (wg_h, wu_h, wd_h)[wi]
        hpar = h % 2
        qn = "pl"
        if h >= 2:
            E.qwait(qn, "t", half_trm[h - 2])
        E.qdma(qn, "wst%d" % hpar, lambda g, w_h=w_h, r_lo=r_lo, hr=hr,
               wfree=wfree, hpar=hpar: g.dma_start(
                   out=wstf[:, hpar * 12288:
                            hpar * 12288 + hr * wfree].rearrange(
                       "p (a b) -> p a b", b=wfree),
                   in_=w_h[r_lo * 128:(r_lo + hr) * 128, :].rearrange(
                       "(c p) e -> p c e", p=128)))
        wst_m[h] = E.mark("wst%d" % hpar)

    if nphase >= 1:
        issue_half_load(0)
        issue_half_load(1)
        for h in range(6):
            wi, hf, ncol, wfree, hr, r_lo = _half_params(h)
            nunit = (hr + 7) // 8
            for cb in range(ncol):
                for u in range(nunit):
                    r0, r1 = u * 8, min(hr, u * 8 + 8)
                    wtasks.append(("g", h, cb, r0, r1, wfree, r_lo, hr))
                wtasks.append(("d", h, cb, hr, r_lo))
            wtasks.append(("l", h))

    def pump(n):
        wT_tab = (wgT_s, wuT_s, wdT_s)
        while n > 0 and wtasks:
            task = wtasks.pop(0)
            if task[0] == "g":
                _, h, cb, r0, r1, wfree, r_lo, hr = task
                hpar = h % 2
                rb = len(rot) % 2
                E.qwait("t", "const", sp_const)
                E.qwait("t", "wst%d" % hpar, wst_m[h])
                if len(rot) >= 2:
                    E.qwait("t", "a", rot[-2])
                half_trm[h] = E.qgroup("t", [
                    (lambda t, r=r, r0=r0, cb=cb, rb=rb,
                     wfree=wfree, hpar=hpar: t.transpose(
                         trA[:, rb * 1024 + (r - r0) * 128:
                             rb * 1024 + (r - r0 + 1) * 128],
                         wstf[:, hpar * 12288 + r * wfree + cb * 128:
                              hpar * 12288 + r * wfree + (cb + 1) * 128],
                         ident[:]))
                    for r in range(r0, r1)], "t")
                E.qwait("a", "t")
                if len(wts_dma) >= 4:
                    E.qwait("a", *wts_dma[-4])
                rot.append(E.qsop("a", lambda a, r0=r0, r1=r1, rb=rb,
                                  wpar=cb % 4: a.activation(
                    out=wts[:, wpar, r0 * 128:r1 * 128],
                    in_=trA[:, rb * 1024:
                            rb * 1024 + (r1 - r0) * 128],
                    func=AF.Copy)))
                n -= 1
            elif task[0] == "d":
                _, h, cb, hr, r_lo = task
                wT_s = wT_tab[halves[h][0]]
                E.qwait("a", "a")
                wts_dma.append(("wTs%d" % (cb % 4), E.qdma(
                    "a", "wTs%d" % (cb % 4), lambda a, cb=cb, hr=hr,
                    r_lo=r_lo, wT_s=wT_s, wpar=cb % 4: a.dma_start(
                        out=wT_s[cb * 128:(cb + 1) * 128,
                                 r_lo * 128:(r_lo + hr) * 128],
                        in_=wts[:, wpar, 0:hr * 128]))))
            else:
                _, h = task
                if h + 2 < 6:
                    issue_half_load(h + 2)
        if not wtasks and not wts_final:
            for kk2 in ("wTs0", "wTs1", "wTs2", "wTs3"):
                wts_final[kk2] = E.mark(kk2)

    wgT_sv = wgT_s.rearrange("(c p) h -> p c h", p=128)
    wuT_sv = wuT_s.rearrange("(c p) h -> p c h", p=128)
    wgu_pref = {}

    def issue_wgu(hb):
        par = hb % 2
        E.qdma("sy", "wgu%d" % par, lambda s, hb=hb, par=par: s.dma_start(
            out=wgu[:, par, 0, :, :],
            in_=wgT_sv[:, :, hb * 128:(hb + 1) * 128]))
        E.qdma("sy", "wgu%d" % par, lambda s, hb=hb, par=par: s.dma_start(
            out=wgu[:, par, 1, :, :],
            in_=wuT_sv[:, :, hb * 128:(hb + 1) * 128]))
        return E.mark("wgu%d" % par)

    if nphase >= 2:
        nq_tiles = 16 if nphase >= 3 else 0

        aggr_m = []

        nmark = []
        entries = [(xk_h, b, True) for b in range(16)]
        if nq_tiles:
            entries += [(xq_h, b, False) for b in range(4)]
        NB = len(entries)

        tr_tm = []
        k_evac = []
        v_evac = []
        kv_tmark = []
        rstd_ms = {}
        ld_ms = {}

        def emit_load(gb):
            src_h, lb, kv = entries[gb]
            bpar = gb % 2
            if gb >= 2:
                E.qwait("sy", "v", nmark[-1])
            E.qdma("sy", "xkb4_%d" % bpar,
                   lambda s, lb=lb, bpar=bpar, src_h=src_h: s.dma_start(
                       out=xkb4[:, bpar, :, :],
                       in_=src_h[lb * 512:(lb + 1) * 512, :].rearrange(
                           "(ti p) d -> p ti d", p=128)))
            ld_ms[gb] = E.mark("xkb4_%d" % bpar)

        def emit_stats(gb):
            bpar = gb % 2
            g4 = 4 * gb
            E.qwait("v", "xkb4_%d" % bpar, ld_ms[gb])
            for ti in range(4):
                b2 = ti % 2
                E.qsop("v", lambda v, bpar=bpar, ti=ti, b2=b2: v.bn_stats(
                    bn6[:, b2, 0:6], xkb4[:, bpar, ti, 0:512]),
                    dep=(aggr_m[-2] if len(aggr_m) >= 2 else 0))
                E.qsop("v", lambda v, bpar=bpar, ti=ti, b2=b2: v.bn_stats(
                    bn6[:, b2, 6:12], xkb4[:, bpar, ti, 512:1024]),
                    dep=-10**9)
                aggr_m.append(E.qsop("v", lambda v, g4=g4, ti=ti, b2=b2:
                                     v.bn_aggr(
                    mvall[:, g4 + ti, :], bn6[:, b2, :])))
            E.qsop("v", lambda v, g4=g4: v.tensor_scalar_add(
                veps[:, g4:g4 + 4], mvall[:, g4:g4 + 4, 1:2], EPS))
            E.qsop("v", lambda v, g4=g4: v.tensor_scalar(
                out=rstdall[:, g4:g4 + 4], in0=veps[:, g4:g4 + 4],
                scalar1=0.5, scalar2=0.5, op0=ALU.mult, op1=ALU.add))
            E.qsop("v", lambda v, g4=g4: v.reciprocal(
                rstdall[:, g4:g4 + 4], rstdall[:, g4:g4 + 4]))
            for _ in range(5):
                E.qsop("v", lambda v, g4=g4: v.tensor_mul(
                    fsb[:, 0, 0:4], rstdall[:, g4:g4 + 4],
                    rstdall[:, g4:g4 + 4]))
                E.qsop("v", lambda v, g4=g4: v.tensor_mul(
                    fsb[:, 0, 0:4], fsb[:, 0, 0:4], veps[:, g4:g4 + 4]))
                E.qsop("v", lambda v: v.tensor_scalar(
                    out=fsb[:, 0, 0:4], in0=fsb[:, 0, 0:4],
                    scalar1=-0.5, scalar2=1.5, op0=ALU.mult, op1=ALU.add))
                E.qsop("v", lambda v, g4=g4: v.tensor_mul(
                    rstdall[:, g4:g4 + 4], rstdall[:, g4:g4 + 4],
                    fsb[:, 0, 0:4]))
            rstd_ms[gb] = E.mark("v")

        xnsm = {}

        def emit_ladder(gb):
            src_h, lb, kv = entries[gb]
            bpar = gb % 2
            pb = gb % 2
            g4 = 4 * gb
            for ti in range(4):
                tile = 4 * lb + ti
                par = ti % 2
                if len(tr_tm) >= 2:
                    E.qwait("v", "t", tr_tm[-2])
                v_m = E.qsop("v", dep=rstd_ms[gb], fn=lambda v,
                             bpar=bpar, ti=ti, par=par,
                             g4=g4: v.tensor_scalar(
                    out=xnr[:, par, :], in0=xkb4[:, bpar, ti, :],
                    scalar1=mvall[:, g4 + ti, 0:1],
                    scalar2=rstdall[:, g4 + ti:g4 + ti + 1],
                    op0=ALU.subtract, op1=ALU.mult))

                if ti == 3:
                    nmark.append(v_m)
                E.qwait("t", "v", v_m)
                rb = len(rot) % 2
                if len(rot) >= 2:
                    E.qwait("t", "a", rot[-2])
                tr_tm.append(E.qgroup("t", [
                    (lambda t, dcc=dcc, par=par, rb=rb: t.transpose(
                        trA[:, rb * 1024 + dcc * 128:
                            rb * 1024 + (dcc + 1) * 128],
                        xnr[:, par, dcc * 128:(dcc + 1) * 128],
                        ident[:]))
                    for dcc in range(DC)], "t"))

                E.qwait("a", "t")
                if kv:
                    fn = (lambda a, pb=pb, ti=ti, rb=rb: a.activation(
                        out=xns[:, pb, :, ti * 128:(ti + 1) * 128],
                        in_=trA[:, rb * 1024:(rb + 1) * 1024]
                        .rearrange("p (c q) -> p c q", q=128),
                        func=AF.Copy))
                else:
                    fn = (lambda a, tile=tile, rb=rb: a.activation(
                        out=xnq[:, :, tile * 128:(tile + 1) * 128],
                        in_=trA[:, rb * 1024:(rb + 1) * 1024]
                        .rearrange("p (c q) -> p c q", q=128),
                        func=AF.Copy))
                rot.append(E.qsop("a", fn))
            xnsm[gb] = E.mark("a")

        def emit_K(gb):
            src_h, lb, kv = entries[gb]
            pb = gb % 2
            a_xns = xnsm[gb]
            for e in range(DC):
                ph = e % 2
                E.qwait("t", "a", a_xns)
                if len(k_evac) >= 2:
                    E.qwait("t", "a", k_evac[-2])
                kv_tmark.append(E.qgroup("t", [
                    (lambda t, dcc=dcc, pb=pb, e=e, ph=ph: t.matmul(
                        psBC[:, ph * 512:(ph + 1) * 512],
                        wk_b[:, dcc, e * 128:(e + 1) * 128],
                        xns[:, pb, dcc, :],
                        start=(dcc == 0), stop=(dcc == DC - 1)))
                    for dcc in range(DC)], "t"))

                E.qwait("a", "t", kv_tmark[-1])
                E.qwait("a", "kst")
                k_evac.append(E.qsop("a", lambda a, e=e, ph=ph:
                                     a.activation(
                    out=kst[:, e, :],
                    in_=psBC[:, ph * 512:(ph + 1) * 512],
                    func=AF.Copy)))
            E.qwait("pl", "a")
            E.qdma("pl", "kst", lambda g, lb=lb: g.dma_start(
                out=kT_s[lb],
                in_=kst.ap().rearrange("p c q -> p (c q)")))

        def emit_V(gb):
            src_h, lb, kv = entries[gb]
            pb = gb % 2
            for sti in range(4):
                vtgt = psD[:, 0:1024] if sti % 2 == 0                     else psA[:, 1024:2048]
                if len(v_evac) >= 2:
                    E.qwait("t", "a", v_evac[-2])
                fns = []
                for dcc in range(DC):
                    for eh in range(2):
                        fns.append(lambda t, dcc=dcc, eh=eh, pb=pb,
                                   sti=sti, vtgt=vtgt: t.matmul(
                            vtgt[:, eh * 512:(eh + 1) * 512],
                            xns[:, pb, dcc,
                                sti * 128:(sti + 1) * 128],
                            wv_b[:, dcc, eh * 512:(eh + 1) * 512],
                            start=(dcc == 0), stop=(dcc == DC - 1)))
                kv_tmark.append(E.qgroup("t", fns, "t"))

                E.qwait("a", "t", kv_tmark[-1])
                E.qwait("a", "vst")
                v_evac.append(E.qsop("a", lambda a, sti=sti,
                                     vtgt=vtgt: a.activation(
                    out=vst[:, sti, :], in_=vtgt, func=AF.Copy)))

            E.qwait("pl", "a")
            E.qdma("pl", "vst", lambda g, lb=lb: g.dma_start(
                out=vv_s[lb],
                in_=vst.ap().rearrange("p t e -> p (t e)")))

        def emit_Q(sb):
            for e in range(DC):
                ph = e % 2
                E.qwait("t", "a", xnsm[16 + sb])
                if len(k_evac) >= 2:
                    E.qwait("t", "a", k_evac[-2])
                kv_tmark.append(E.qgroup("t", [
                    (lambda t, dcc=dcc, sb=sb, e=e, ph=ph: t.matmul(
                        psBC[:, ph * 512:(ph + 1) * 512],
                        wq_b[:, dcc, e * 128:(e + 1) * 128],
                        xnq[:, dcc, sb * 512:(sb + 1) * 512],
                        start=(dcc == 0), stop=(dcc == DC - 1)))
                    for dcc in range(DC)], "t"))
                E.qwait("a", "t", kv_tmark[-1])
                k_evac.append(E.qsop("a", lambda a, sb=sb, e=e, ph=ph:
                                     a.activation(
                    out=qT[:, e, sb * 512:(sb + 1) * 512],
                    in_=psBC[:, ph * 512:(ph + 1) * 512],
                    func=AF.Copy)))

        E.qwait("t", "wqkv")
        emit_load(0)
        emit_stats(0)
        emit_load(1)
        emit_stats(1)
        emit_ladder(0)
        emit_load(2)
        for gb in range(NB):
            kv = entries[gb][2]
            boundary = kv and gb + 1 < NB and not entries[gb + 1][2]
            if kv:
                emit_K(gb)
            elif nphase >= 4:
                emit_Q(gb - 16)
            if gb >= 2:
                pump(5)
            if not boundary:
                if gb + 1 < NB:
                    emit_ladder(gb + 1)
                if gb + 3 < NB:
                    emit_load(gb + 3)
                if gb + 2 < NB:
                    emit_stats(gb + 2)
            if kv:
                emit_V(gb)
            if gb >= 2:
                pump(5)
            if boundary:
                emit_ladder(gb + 1)
                if gb + 3 < NB:
                    emit_load(gb + 3)
                if gb + 2 < NB:
                    emit_stats(gb + 2)
        pump(10**6)

        if nphase >= 5:
            for kk2 in ("wTs0", "wTs1", "wTs2", "wTs3"):
                E.qwait("sy", kk2, wts_final[kk2])
            wgu_pref[(0, 1)] = issue_wgu(1)
            E.qwait("sy", "vst")
            wgu_pref[(0, 0)] = issue_wgu(0)

    E.barrier()

    E.barrier()

    if nphase >= 5:
        E.qdma("sy", "wdT", lambda s: s.dma_start(
            out=wdT_b[:], in_=wdT_s.rearrange("(c p) e -> p c e", p=128)))
        sp_wd = E.mark("wdT")

        gu_tmark = []
        hm_vmark = []
        si_amark = []
        g_amark = []
        dn_tmark = []
        for q in range(2):
            qs = q * 1024
            for hb in range(HC):
                par = hb % 2
                if (q, hb) in wgu_pref:
                    sp_m = wgu_pref.pop((q, hb))
                else:
                    if len(gu_tmark) >= 2:
                        E.qwait("sy", "t", gu_tmark[-2][1])
                    sp_m = issue_wgu(hb)

                gb = (hb % 2) * 1024
                upD = hb % 2 == 1
                E.qwait("t", "wgu%d" % par, sp_m)
                if len(si_amark) >= 2:
                    E.qwait("t", "a", si_amark[-2])
                if len(hm_vmark) >= 2:
                    E.qwait("t", "v", hm_vmark[-2])
                if upD and g_amark:
                    E.qwait("t", "a", g_amark[-1])
                fns = []
                for dcc in range(DC):
                    for sh in range(2):
                        fns.append(lambda t, dcc=dcc, sh=sh, qs=qs, par=par,
                                   gb=gb: t.matmul(
                            psA[:, gb + sh * 512:gb + (sh + 1) * 512],
                            wgu[:, par, 0, dcc, :],
                            xnq[:, dcc, qs + sh * 512:qs + (sh + 1) * 512],
                            start=(dcc == 0), stop=(dcc == DC - 1)))
                m_g = E.qgroup("t", fns, "t")
                fns = []
                for dcc in range(DC):
                    for sh in range(2):
                        fns.append(lambda t, dcc=dcc, sh=sh, qs=qs, par=par,
                                   upD=upD: t.matmul(
                            (psD if upD else psBC)[:, sh * 512:
                                                   (sh + 1) * 512],
                            wgu[:, par, 1, dcc, :],
                            xnq[:, dcc, qs + sh * 512:qs + (sh + 1) * 512],
                            start=(dcc == 0), stop=(dcc == DC - 1)))
                m_u = E.qgroup("t", fns, "t")
                gu_tmark.append((m_g, m_u))

                E.qwait("a", "t", m_g)
                if hm_vmark:
                    E.qwait("a", "v", hm_vmark[-1])
                si_amark.append(E.qsop("a", lambda a, gb=gb: a.activation(
                    out=silu_t[:], in_=psA[:, gb:gb + 1024],
                    func=AF.Silu, scale=SCALE)))

                E.qwait("v", "t", m_u)
                E.qwait("v", "a", si_amark[-1])
                hm_vmark.append(E.qsop("v", lambda v, hb=hb, upD=upD:
                                       v.tensor_tensor(
                    hact[:, hb, :], silu_t[:], (psD if upD else psBC)[:],
                    ALU.mult)))
            v_hact = E.mark("v")

            for sti in range(8):
                E.qwait("t", "v", v_hact)
                E.qwait("t", "wdT", sp_wd)
                if len(g_amark) >= 1:
                    E.qwait("t", "a", g_amark[-1])
                fns = []
                for hb in range(HC):
                    for eh in range(2):
                        fns.append(lambda t, hb=hb, sti=sti, eh=eh: t.matmul(
                            psD[:, eh * 512:(eh + 1) * 512],
                            hact[:, hb, sti * 128:(sti + 1) * 128],
                            wdT_b[:, hb, eh * 512:(eh + 1) * 512],
                            start=(hb == 0), stop=(hb == HC - 1)))
                dn_tmark.append(E.qgroup("t", fns, "t"))

                E.qwait("a", "t", dn_tmark[-1])
                g_amark.append(E.qsop("a", lambda a, q=q, sti=sti:
                                      a.activation(
                    out=g_b[:, q * 8 + sti, :], in_=psD[:],
                    func=AF.Silu, scale=SCALE * SCALE)))

    E.barrier()

    if nphase >= 6:
        cmb_vm = []
        kbuf_last = {}
        vread_m = {}
        pt_last = {}
        score_free = {}
        trb_free = {}
        obank_free = []
        lin_prev = [0]

        for pair in range(NS // 2):
            kA, kB = 2 * pair, 2 * pair + 1
            nsupA = kA + 1
            nsupB = kB + 1
            pbufs = (pbufA, pbufB)
            tr_m = {}
            pte_m = {}
            exp_m = {}
            av_m = {}
            vdma_m = {}

            for sl, kk in ((0, kA), (1, kB)):
                if cmb_vm:
                    E.qwait("pl", "v", cmb_vm[-1])
                E.qdma("pl", "xrs%d" % (kk % 2), lambda g, kk=kk: g.dma_start(
                    out=xrs[:, kk % 2, :],
                    in_=xq_h[kk * 128:(kk + 1) * 128, :]))
            xr_m = (E.mark("xrs0"), E.mark("xrs1"))

            def slots_of(s):
                return [0, 1] if s < nsupA else [1]

            def emit_tr(s):
                tb = 2048 + (s % 2) * 1024
                slots = slots_of(s)
                for sl in slots:
                    E.qwait("t", "a", exp_m[(sl, s)])
                fw = trb_free.get(s % 2)
                if fw is not None:
                    E.qwait("t", "a", fw)
                tr_m[s] = E.qgroup("t", [
                    (lambda t, sl=sl, kt=kt, s=s, tb=tb: t.transpose(
                        trA[:, tb + sl * 512 + kt * 128:
                            tb + sl * 512 + (kt + 1) * 128],
                        pbufs[sl][:, s * 512 + kt * 128:
                                  s * 512 + (kt + 1) * 128],
                        ident[:]))
                    for sl in slots for kt in range(4)], "t")

            def emit_pte(s):
                ph = s % 2
                tb = 2048 + (s % 2) * 1024
                s0 = 0 if s < nsupA else 1
                E.qwait("a", "t", tr_m[s])
                pw = pt_last.get(ph)
                if pw is not None:
                    E.qwait("a", "t", pw)
                pte_m[s] = E.qsop("a", lambda a, ph=ph, tb=tb, s0=s0:
                                  a.activation(
                    out=pT[:, ph, s0:2, :, :],
                    in_=trA[:, tb + s0 * 512:tb + 1024]
                    .rearrange("p (s t q) -> p s t q", t=4, q=128),
                    func=AF.Copy))
                trb_free[s % 2] = pte_m[s]

            def emit_av(s):
                ph = s % 2
                E.qwait("t", "a", pte_m[s])
                E.qwait("t", "vbuf%d" % ph, vdma_m[s])
                if s == 0:
                    for bw in obank_free:
                        E.qwait("t", "v", bw)
                    del obank_free[:]
                fns = []
                for sl in slots_of(s):
                    tgt = psBC if sl == 0 else psD
                    nv = nsupA if sl == 0 else nsupB
                    for kt in range(4):
                        for eh in range(2):
                            fns.append(lambda t, sl=sl, kt=kt, eh=eh, s=s,
                                       ph=ph, tgt=tgt, nv=nv: t.matmul(
                                tgt[:, eh * 512:(eh + 1) * 512],
                                pT[:, ph, sl, kt, :],
                                vbuf[:, ph, kt, eh * 512:(eh + 1) * 512],
                                start=(s == 0 and kt == 0),
                                stop=(s == nv - 1 and kt == 3)))
                av_m[s] = E.qgroup("t", fns, "t")
                vread_m[ph] = av_m[s]
                pt_last[ph] = av_m[s]

            for s in range(nsupB):
                ph4 = s % 4
                kb_t = kbuf if ph4 < 2 else kbufX
                kb_ap = kb_t[:, ph4 % 2, :, :]
                kb_w = kbuf_last.get(ph4)
                if kb_w is not None:
                    E.qwait("sy", "t", kb_w)
                E.qdma("sy", "kbuf%d" % ph4, lambda s_, sup=s, kb_ap=kb_ap:
                       s_.dma_start(
                           out=kb_ap,
                           in_=kT_s[sup].rearrange("p (c q) -> p c q",
                                                   q=512)))
                kdma = E.mark("kbuf%d" % ph4)

                vph = s % 2
                vw = vread_m.get(vph)
                if vw is not None:
                    E.qwait("pl", "t", vw)
                E.qdma("pl", "vbuf%d" % vph, lambda g, sup=s, vph=vph:
                       g.dma_start(
                           out=vbuf[:, vph, :, :],
                           in_=vv_s[sup].rearrange("p (t e) -> p t e",
                                                   e=DM)))
                vdma_m[s] = E.mark("vbuf%d" % vph)

                if s >= 1:
                    emit_tr(s - 1)

                slots = slots_of(s)
                E.qwait("t", "kbuf%d" % ph4, kdma)
                for sl in slots:
                    fw = score_free.get(sl)
                    if fw is not None:
                        E.qwait("t", "a", fw)
                fns = []
                for sl in slots:
                    kk = kA if sl == 0 else kB
                    for e in range(DC):
                        fns.append(lambda t, e=e, kk=kk, sl=sl, kb_t=kb_t,
                                   php=ph4 % 2: t.matmul(
                            psA[:, sl * 512:(sl + 1) * 512],
                            qT[:, e, kk * 128:(kk + 1) * 128],
                            kb_t[:, php, e, :],
                            start=(e == 0), stop=(e == DC - 1)))
                qk_m = E.qgroup("t", fns, "t")
                kbuf_last[ph4] = qk_m

                msk_m = {}
                for sl in slots:
                    if s == (nsupA if sl == 0 else nsupB) - 1:
                        E.qwait("v", "t", qk_m)
                        msk_m[sl] = E.qsop("v", lambda v, sl=sl:
                                           v.tensor_tensor(
                            psA[:, sl * 512:(sl + 1) * 512],
                            psA[:, sl * 512:(sl + 1) * 512],
                            maskb[:], ALU.add))

                if s >= 1:
                    emit_pte(s - 1)

                E.qwait("a", "t", qk_m)
                if s == 0:
                    E.qwait("a", "v", lin_prev[0])
                for sl in slots:
                    if sl in msk_m:
                        E.qwait("a", "v", msk_m[sl])
                    exp_m[(sl, s)] = E.qsop("a", lambda a, sl=sl, sup=s:
                                            a.activation(
                        out=pbufs[sl][:, sup * 512:(sup + 1) * 512],
                        in_=psA[:, sl * 512:(sl + 1) * 512],
                        func=AF.Exp, scale=1.0 / 32.0,
                        accum_out=lsb[:, sl, sup:sup + 1]))
                    score_free[sl] = exp_m[(sl, s)]

                if s >= 1:
                    emit_av(s - 1)

            emit_tr(nsupB - 1)
            emit_pte(nsupB - 1)
            emit_av(nsupB - 1)

            E.qwait("v", "a")
            for sl, nsup in ((0, nsupA), (1, nsupB)):
                E.qsop("v", lambda v, sl=sl, nsup=nsup: v.reduce_sum(
                    lsum[:, sl:sl + 1], lsb[:, sl, 0:nsup], axis=AX.X))
            lin_prev[0] = E.qsop("v", lambda v: v.reciprocal(linv[:],
                                                            lsum[:]))

            for sl, kk in ((0, kA), (1, kB)):
                av_last = av_m[(nsupA if sl == 0 else nsupB) - 1]
                E.qwait("v", "t", av_last)
                E.qwait("v", "xrs%d" % (kk % 2), xr_m[kk % 2])
                E.qwait("v", "out%d" % (kk % 2))
                src_ps = psBC if sl == 0 else psD
                E.qsop("v", lambda v, kk=kk, sl=sl, src_ps=src_ps:
                       v.scalar_tensor_tensor(
                    out=sg_t[:], in0=src_ps[:], scalar=linv[:, sl:sl + 1],
                    in1=g_b[:, kk, :], op0=ALU.mult, op1=ALU.mult))
                m = E.qsop("v", lambda v, kk=kk: v.tensor_add(
                    outt[:, kk % 2, :], sg_t[:], xrs[:, kk % 2, :]))
                cmb_vm.append(m)
                obank_free.append(m)

                E.qwait("pl", "v", m)
                E.qdma("pl", "out%d" % (kk % 2), lambda g, kk=kk:
                       g.dma_start(
                    out=out_h[kk * 128:(kk + 1) * 128, :],
                    in_=outt[:, kk % 2, :]))

        E.qwait("pl", "out0")
        E.qwait("pl", "out1")

    with nc.Block() as blk:
        E.flush(blk)
    E.es.close()
    return nc


_NC_CACHE = {}


def _mk_inputs(X, wq, wk, wv, wg, wu, wd):
    ident = np.eye(128, dtype=np.float32).astype(BF)
    in_maps = []
    for core in range(8):
        b, j = core // 4, core % 4
        rows = np.arange(NS) * 4 + j
        xq = np.ascontiguousarray(
            X[b].reshape(64, 128, DM)[rows].reshape(QR, DM))
        c = np.arange(512)[None, :]
        r = np.arange(128)[:, None]
        mask = np.where(c <= 128 * j + r, 0.0, NEG).astype(np.float32)
        in_maps.append({
            "xk": X[b], "xq": xq,
            "wq": wq, "wk": wk, "wv": wv,
            "wg": wg, "wu": wu, "wd": wd,
            "mask": mask, "ident": ident,
        })
    return in_maps


def kernel(L0_attn_X, L0_attn_Wq, L0_attn_Wk, L0_attn_Wv,
           L0_ffn_w_gate, L0_ffn_w_up, L0_ffn_w_down):
    X = np.ascontiguousarray(np.asarray(L0_attn_X, dtype=np.float32))
    wq = np.ascontiguousarray(np.asarray(L0_attn_Wq, dtype=np.float32))
    wk = np.ascontiguousarray(np.asarray(L0_attn_Wk, dtype=np.float32))
    wv = np.ascontiguousarray(np.asarray(L0_attn_Wv, dtype=np.float32))
    wg = np.ascontiguousarray(np.asarray(L0_ffn_w_gate, dtype=np.int32))
    wu = np.ascontiguousarray(np.asarray(L0_ffn_w_up, dtype=np.int32))
    wd = np.ascontiguousarray(np.asarray(L0_ffn_w_down, dtype=np.int32))

    if "nc" not in _NC_CACHE:
        _NC_CACHE["nc"] = build()
    nc = _NC_CACHE["nc"]
    in_maps = _mk_inputs(X, wq, wk, wv, wg, wu, wd)
    res = run_bass_kernel_spmd(nc, in_maps, core_ids=list(range(8)))

    out = np.empty((2, S, DM), dtype=np.float32)
    for core in range(8):
        b, j = core // 4, core % 4
        rows = np.arange(NS) * 4 + j
        out[b].reshape(64, 128, DM)[rows] = \
            res.results[core]["out"].reshape(NS, 128, DM)
    return out
